# version 02 (final)
# baseline (speedup 1.0000x reference)
import math
from contextlib import ExitStack

import numpy as np

import concourse.bass as bass
import concourse.mybir as mybir
from concourse.bass_utils import run_bass_kernel_spmd

F32 = mybir.dt.float32
BF16 = mybir.dt.bfloat16
U8 = mybir.dt.uint8
AF = mybir.ActivationFunctionType
ALU = mybir.AluOpType
AX = mybir.AxisListType

MAGIC = 3 * 2.0 ** 22
NCORES = 8
T = 2048
D = 1024
NT = T // 128
KC = D // 128
HEADS, HD = 16, 64
DI = 3072
NEG = -30000.0

ENGINES = ("sync", "gpsimd", "tensor", "vector", "scalar")


_FLUSH = [None]


def _vgap(e, n=2):
    fl = _FLUSH[0]
    if fl is not None:
        for _ in range(n):
            e.tensor_scalar(fl, fl, 1.0, None, mybir.AluOpType.mult)
    e.nop()


def _rsum(e, *a, **k):
    _vgap(e)
    ins = e.reduce_sum(*a, **k)
    _vgap(e, 4)
    return ins


def _rmax(e, *a, **k):
    _vgap(e)
    ins = e.reduce_max(*a, **k)
    _vgap(e, 4)
    return ins


class Prog:

    def __init__(self, nc, ctx):
        self.nc = nc
        self.ctx = ctx
        self.ops = {e: [] for e in ENGINES}
        self.tot = {}
        self.lastwait = {e: {} for e in ENGINES}
        self.sems = {}
        for name in ("t", "v", "s", "g", "cc"):
            self.sems[name] = self._mk("ms_" + name)
        self.vflush = None
        self.sems["dsync"] = self._mk("dma_sync")
        self.sems["dg"] = self._mk("dma_g")
        self.sems["dsc"] = self._mk("dma_sc")

    def _mk(self, name):
        s = self.ctx.enter_context(self.nc.semaphore(name))
        self.tot[s] = 0
        return s

    def emit(self, eng, fn, waits=(), inc=None, n=1):
        bysem = {}
        for s, v in waits:
            if v > bysem.get(s, 0):
                bysem[s] = v
        wl = []
        for s, v in bysem.items():
            assert v <= self.tot[s], f"wait on future token ({v} > {self.tot[s]})"
            if self.lastwait[eng].get(s, 0) >= v:
                continue
            self.lastwait[eng][s] = v
            wl.append((s, v))

        def op(engobj, wl=tuple(wl), fn=fn, inc=inc, n=n):
            for s, v in wl:
                engobj.wait_ge(s, v)
            ins = fn(engobj)
            if inc is not None:
                assert ins is not None, "inc requested but fn returned no instruction"
                ins.then_inc(inc, n)

        self.ops[eng].append(op)
        if inc is not None:
            self.tot[inc] += n
            return (inc, self.tot[inc])
        return None

    def step(self, eng, fn, waits=()):
        key = {"tensor": "t", "vector": "v", "scalar": "s", "gpsimd": "g"}[eng]
        if eng == "vector" and self.vflush is not None:
            self.emit(eng, fn, waits, inc=None)
            fl = self.vflush
            return self.emit(eng, (lambda e, fl=fl: e.tensor_scalar(
                fl, fl, 1.0, None, mybir.AluOpType.mult)), (), inc=self.sems[key], n=1)
        return self.emit(eng, fn, waits, inc=self.sems[key], n=1)

    def vspace(self, n=4):
        for _ in range(n):
            self.emit("vector", (lambda e, fl=self.vflush: e.tensor_scalar(
                fl, fl, 1.0, None, mybir.AluOpType.mult)))

    def run(self, eng, fn, waits=()):
        return self.emit(eng, fn, waits, inc=None)

    def dma(self, eng, out, in_, waits=(), slow=False):
        sem = {"sync": self.sems["dsync"], "gpsimd": self.sems["dg"],
               "scalar": self.sems["dsc"]}[eng]
        return self.emit(eng, lambda e, o=out, i=in_, sl=slow: e.dma_start(
            out=o, in_=i, allow_slow_non_contiguous=sl), waits, inc=sem, n=16)

    def realize(self):
        block = self.ctx.enter_context(self.nc.Block())
        for name in ENGINES:
            ops = self.ops[name]
            if not ops:
                continue
            getattr(block, name)(lambda e, ops=ops: [op(e) for op in ops])


class Alloc:

    def __init__(self, arena, nbytes):
        self.arena = arena
        self.nbytes = nbytes
        self.blocks = []

    def alloc(self, tag, size, align=64):
        cur = 0
        placed = None
        for boff, bsize, _ in sorted(self.blocks):
            ca = (cur + align - 1) // align * align
            if ca + size <= boff:
                placed = ca
                break
            cur = max(cur, boff + bsize)
        if placed is None:
            placed = (cur + align - 1) // align * align
        assert placed + size <= self.nbytes, (
            f"SBUF arena overflow: {tag} needs {size} at {placed}, cap {self.nbytes}; "
            f"live: {[(t, o, s) for o, s, t in sorted(self.blocks)]}")
        self.blocks.append((placed, size, tag))
        return Buf(self, placed, size, tag)

    def free(self, buf):
        self.blocks.remove((buf.off, buf.size, buf.tag))


class Buf:
    def __init__(self, al, off, size, tag):
        self.al, self.off, self.size, self.tag = al, off, size, tag

    def view(self, shape, dtype, byte_off=0):
        esz = mybir.dt.size(dtype)
        free = 1
        for d in shape[1:]:
            free *= d
        need = free * esz
        assert byte_off + need <= self.size, (self.tag, shape, byte_off, self.size)
        ap = self.al.arena.ap()[:shape[0], self.off + byte_off:self.off + byte_off + need]
        ap = ap.bitcast(dtype)
        if len(shape) > 2:
            dims = " ".join(f"d{i}" for i in range(1, len(shape)))
            kw = {f"d{i}": shape[i] for i in range(1, len(shape) - 1)}
            ap = ap.rearrange(f"p ({dims}) -> p {dims}", **kw)
        return ap

    def free(self):
        self.al.free(self)


def ts_op(out, in0, s1, s2, op0, op1=None, accum=None):
    def f(e):
        return e.tensor_scalar(out, in0, s1, s2, op0, op1 if op1 is not None else ALU.bypass,
                               accum_out=accum)
    return f


class Build:

    def __init__(self, nc, ctx, upto=None):
        self.nc = nc
        self.ctx = ctx
        self.p = Prog(nc, ctx)
        self.upto = upto
        self.taps = {}
        arena_bytes = (nc.sbuf_bytes_remaining - 4096) // 256 * 256
        self.arena = ctx.enter_context(nc.sbuf_tensor("arena", [128, arena_bytes], U8))
        self.al = Alloc(self.arena, arena_bytes)
        fb = self.al.alloc("VFLUSH", 64)
        self.p.vflush = fb.view([128, 1], F32)
        _FLUSH[0] = self.p.vflush
        self.p.emit("vector", (lambda e, fl=self.p.vflush: e.memset(fl, 1.0)))
        self.ps = [ctx.enter_context(nc.psum_tensor(f"ps{i}", [128, 512], F32))
                   for i in range(6)]
        self.pt = [ctx.enter_context(nc.psum_tensor(f"pt{i}", [128, 1024], BF16))
                   for i in range(2)]
        self.ps_rr = 0
        self.pt_rr = 0
        self.ps_guard = [[] for _ in range(6)]
        self.pt_guard = [None] * 2
        self.inputs = {}
        self.dram = {}

    def din(self, name, shape, dtype=F32):
        t = self.nc.dram_tensor(name, list(shape), dtype, kind="ExternalInput")
        self.dram[name] = t
        return t

    def dout(self, name, shape, dtype=F32):
        t = self.nc.dram_tensor(name, list(shape), dtype, kind="ExternalOutput")
        self.dram[name] = t
        return t

    def dint(self, name, shape, dtype=F32, addr_space="Local"):
        t = self.nc.dram_tensor(name, list(shape), dtype, kind="Internal",
                                addr_space=addr_space)
        self.dram[name] = t
        return t

    def next_ps(self, n=1):
        out = []
        for _ in range(n):
            out.append(self.ps[self.ps_rr % 6])
            self.ps_rr += 1
        return out if n > 1 else out[0]

    def next_pt(self):
        t = self.pt[self.pt_rr % 2]
        self.pt_rr += 1
        return t

    def replicate_row(self, dst_buf, src_ap, width_bytes, waits=(), byte_off=0,
                      have_row0=False):
        p = self.p
        o = dst_buf.off + byte_off
        dst_row = self.arena.ap()[:, o:o + width_bytes].bitcast(U8)
        if have_row0:
            tok = None
            for w_ in waits:
                pass
            tok = p.dma("gpsimd", dst_row[1:2, :], dst_row[0:1, :], waits)
            n = 2
        else:
            tok = p.dma("gpsimd", dst_row[0:1, :], src_ap.bitcast(U8), waits)
            n = 1
        while n < 128:
            tok = p.dma("gpsimd", dst_row[n:min(2 * n, 128), :],
                        dst_row[0:min(n, 128 - n), :], [tok])
            n *= 2
        return tok

    def declare_io(self):
        self.din("x", [T, D])
        self.out_t = self.dout("out", [T, D])
        self.din("L0_c", [1, D])
        self.din("L0_wmod", [D, 6 * D])
        self.din("W_qk", [D, 2 * D])
        self.din("W_v", [D, D])
        self.din("W_out", [D, D])
        for nm in ("q_scale", "q_bias", "k_scale", "k_bias"):
            self.din(nm, [1, HD])
        for pre in ("fq", "fk", "fv"):
            self.din(pre + "_nw", [15, D])
            self.din(pre + "_nb", [1, 15])
            self.din(pre + "_w1", [D, D])
            self.din(pre + "_b1", [1, 16 * HD])
            self.din(pre + "_w2", [16 * HD, D])
            self.din(pre + "_b2", [16, D])
        self.din("L1_c", [1, D])
        self.din("L1_wmod", [D, 6 * D])
        for nm in ("Wq", "Wk", "Wv", "Wo"):
            self.din(nm, [D, D])
        for nm in ("bq", "bk", "bv", "bo"):
            self.din(nm, [1, D])
        self.din("sample_idx", [1, T], mybir.dt.int32)
        self.din("w_gate", [DI, D])
        self.din("w_up", [DI, D])
        self.din("w_down", [D, DI])
        self.dint("hb_up", [T, DI])
        self.dint("hb_gate", [T, DI])
        self.dint("hb_p2", [T, DI])
        self.dint("hb_p3", [T, DI])
        self.dint("cc_in", [1, 8])
        self.dint("cc_out", [1, 8])

    def load_x(self):
        p = self.p
        self.dint("xres", [T, D])
        self.x_src = "x"
        self.x_tile_tok = [None] * NT
        self.CONST = self.al.alloc("CONST", 1024)
        self.idn = self.CONST.view([128, 128], BF16, 0)
        self.tri = self.CONST.view([128, 128], BF16, 256)
        self.eps5 = self.CONST.view([128, 1], F32, 512)
        self.eps6 = self.CONST.view([128, 1], F32, 520)
        def mk(e):
            e.memset(self.idn, 0.0)
            e.affine_select(out=self.idn, in_=self.idn, compare_op=ALU.not_equal,
                            fill=1.0, base=0, pattern=[[-1, 128]], channel_multiplier=1)
            e.memset(self.tri, 0.0)
            e.memset(self.eps5, 1e-5)
            e.memset(self.eps6, 1e-6)
            return e.affine_select(out=self.tri, in_=self.tri, compare_op=ALU.is_ge,
                                   fill=NEG, base=0, pattern=[[-1, 128]],
                                   channel_multiplier=1)
        self.const_tok = p.step("gpsimd", mk)


    def mod(self, layer, extra_waits=()):
        p = self.p
        ew = list(extra_waits)
        cname = f"L{layer}_c"
        wname = f"L{layer}_wmod"
        SCR = self.al.alloc("modscr", 78 * 1024)
        cfm = SCR.view([128, KC], F32, 56 * 1024)
        shi = SCR.view([128, KC], BF16, 57 * 1024)
        slo = SCR.view([128, KC], BF16, 57 * 1024 + 512)
        slf = SCR.view([128, KC], F32, 57 * 1024 + 1024)
        tok_c = p.dma("sync", cfm, self.dram[cname].ap().rearrange(
            "o (k p) -> p (o k)", p=128), ew, slow=True)
        t_sil = p.step("scalar", lambda e: e.activation(cfm, cfm, AF.Silu), [tok_c])
        t_hi = p.step("vector", lambda e: e.tensor_copy(shi, cfm), [t_sil])
        t_lo = p.step("vector", lambda e: e.tensor_tensor(slf, cfm, shi, ALU.subtract), [t_hi])
        t_lo = p.step("vector", lambda e: e.tensor_copy(slo, slf), [t_lo])
        self.MODW = self.al.alloc("modw", 6 * D * 4)
        modw = self.MODW.view([1, 6 * D], F32)
        wsrc = self.dram[wname].ap().rearrange("(k p) n -> p k n", p=128)
        hi_toks = [None, None]
        f32_toks = [None, None]
        lo_tok = None
        wlo_busy = None
        ev_toks = []
        CORR = (4, 5, 10, 11)
        for g in range(12):
            whi = SCR.view([128, KC, 512], BF16, (g % 2) * 8192)
            w = [hi_toks[g % 2]] if hi_toks[g % 2] else list(ew)
            hi_toks[g % 2] = p.dma("gpsimd", whi, wsrc[:, :, g * 512:(g + 1) * 512], w)
            mmw = [hi_toks[g % 2], t_hi]
            if g in CORR:
                wf = SCR.view([128, KC, 512], F32, 24 * 1024 + (g % 2) * 16384)
                wv = [f32_toks[g % 2]] if f32_toks[g % 2] else list(ew)
                f32_toks[g % 2] = p.dma("sync", wf, wsrc[:, :, g * 512:(g + 1) * 512], wv)
                wlo = SCR.view([128, KC, 512], BF16, 16 * 1024)
                wsub = [f32_toks[g % 2], hi_toks[g % 2]]
                if wlo_busy is not None:
                    wsub.append(wlo_busy)
                lo_tok = p.step("vector", (lambda e, wf=wf, wh=whi, wl=wlo:
                                e.tensor_tensor(wl, wf, wh, ALU.subtract)), wsub)
                mmw.append(lo_tok)
                mmw.append(t_lo)
            bank = self.next_ps()
            mmw += self.ps_guard[self.ps.index(bank)]
            terms = [(shi, whi)]
            if g in CORR:
                terms += [(shi, wlo), (slo, whi)]
            nmm = len(terms) * KC
            i = 0
            for sv, wv_ in terms:
                for k in range(KC):
                    tok = p.step("tensor", (lambda e, b=bank, k=k, sv=sv, wv_=wv_,
                                 st=(i == 0), sp=(i == nmm - 1):
                                 e.matmul(b[0:1, :], sv[:, k:k + 1], wv_[:, k, :].opt(),
                                          start=st, stop=sp)), mmw if i == 0 else ())
                    i += 1
            ev = p.step("vector", (lambda e, b=bank, g=g:
                        e.tensor_copy(modw[:, g * 512:(g + 1) * 512], b[0:1, :])), [tok])
            self.ps_guard[self.ps.index(bank)] = [ev]
            hi_toks[g % 2] = tok
            if g in CORR:
                f32_toks[g % 2] = tok
                wlo_busy = tok
            ev_toks.append(ev)
        post = []
        lastv = ev_toks[-1]
        for idx in (1, 4):
            post.append(p.step("vector", (lambda e, i=idx: e.tensor_scalar(
                modw[:, i * D:(i + 1) * D], modw[:, i * D:(i + 1) * D], 1.0, None, ALU.add)),
                [lastv]))
        for j, idx in enumerate((2, 5)):
            sl = modw[:, idx * D:(idx + 1) * D]
            sgn = SCR.view([1, D], F32, 58 * 1024 + j * 8192)
            ab = SCR.view([1, D], F32, 58 * 1024 + j * 8192 + 4096)
            tsg = p.step("vector", (lambda e, sl=sl, gs=sgn: e.tensor_scalar(
                gs, sl, 0.0, None, ALU.is_ge)), [lastv])
            tsg = p.step("vector", (lambda e, gs=sgn: e.tensor_scalar(
                gs, gs, 2.0, -1.0, ALU.mult, ALU.add)), [tsg])
            tab = p.step("scalar", (lambda e, sl=sl, a=ab: e.activation(a, sl, AF.Abs)), [lastv])
            tmx = p.step("vector", (lambda e, a=ab: e.tensor_scalar(
                a, a, 0.25, None, ALU.max)), [tab])
            post.append(p.step("vector", (lambda e, sl=sl, gs=sgn, a=ab: e.tensor_tensor(
                sl, gs, a, ALU.mult)), [tmx, tsg]))
        if not hasattr(self, "MODC"):
            self.MODC = self.al.alloc("MODC", 4096)
        modc = self.MODC.view([6, D], F32)
        tok = p.dma("gpsimd", modc, modw, post)
        self.mod_tok = [tok]
        SCR.free()
        self.MODW.free()

    def mod_rep(self, idx, slot=None, scale=None, extra_waits=()):
        p = self.p
        buf = self.al.alloc(f"modrep{idx}", 4096)
        src = self.arena.ap()[idx:idx + 1,
                              self.MODC.off:self.MODC.off + 4096].bitcast(F32)
        v = buf.view([128, D], F32)
        tok = self.replicate_row(buf, src, 4096,
                                 list(self.mod_tok) + list(extra_waits))
        if scale is not None:
            tok = p.step("vector", (lambda e, v=v, sc=scale: e.tensor_scalar(
                v, v, sc, None, ALU.mult)), [tok])
        return buf, v, tok

    def ln_mod(self, ca_idx, sa_idx, eps=1e-5, consumer=None, out_fm=True):
        p = self.p
        lastx = [tk for tk in self.x_tile_tok if tk]
        ca_buf, ca_rep, t_ca = self.mod_rep(ca_idx, extra_waits=lastx)
        sa_buf, sa_rep, t_sa = self.mod_rep(sa_idx, extra_waits=lastx)
        SC = self.al.alloc("lnscr", 2 * 4096 + 2 * 2048 + 1024)
        HFM = None
        if out_fm and consumer is None:
            HFM = self.al.alloc("HFM", 32 * 1024)
            hfm = HFM.view([128, KC, T], BF16)
        ev_toks = []
        xsrc = self.dram[self.x_src].ap().rearrange("(t p) d -> p t d", p=128)
        XT = self.al.alloc("xtiles", 3 * 4096)
        xt_guard = {}
        sq_guard = {}
        htm_guard = {}
        for t in range(NT):
            xt = XT.view([128, D], F32, (t % 3) * 4096)
            xw = [self.x_tile_tok[t]] if self.x_tile_tok[t] else []
            xw += xt_guard.get(t % 3, []) + list(self.mod_tok)
            xw = [p.dma("sync", xt, xsrc[:, t, :], xw)]
            sq = SC.view([128, D], F32, (t % 2) * 4096)
            st = SC.view([128, 16], F32, 2 * 4096 + (t % 2) * 512)
            htm = SC.view([128, D], BF16, 2 * 4096 + 1024 + (t % 2) * 2048)
            sqw = [sq_guard[t % 2]] if (t % 2) in sq_guard else []
            t_sq = p.step("vector", (lambda e, x=xt, q=sq: e.tensor_tensor(
                q, x, x, ALU.mult)), xw + sqw)
            p.run("vector", (lambda e, x=xt, s=st: _rsum(e, s[:, 0:1], x, axis=AX.X)), xw)
            p.run("vector", (lambda e, q=sq, s=st: _rsum(e, s[:, 1:2], q, axis=AX.X)), [t_sq])
            p.run("vector", (lambda e, s=st: e.tensor_scalar(s[:, 2:3], s[:, 0:1], 1.0 / D, None, ALU.mult)))
            p.run("vector", (lambda e, s=st: e.tensor_scalar(s[:, 3:4], s[:, 1:2], 1.0 / D, None, ALU.mult)))
            p.run("vector", (lambda e, s=st: e.tensor_tensor(s[:, 4:5], s[:, 2:3], s[:, 2:3], ALU.mult)))
            tv = p.step("vector", (lambda e, s=st: e.tensor_tensor(s[:, 3:4], s[:, 3:4], s[:, 4:5], ALU.subtract)))
            tsd = p.step("scalar", (lambda e, s=st: e.activation(
                s[:, 5:6], s[:, 3:4], AF.Sqrt, bias=self.eps5)), [tv, self.const_tok])
            tr = p.step("vector", (lambda e, s=st: e.reciprocal(s[:, 6:7], s[:, 5:6])), [tsd])
            p.vspace(4)
            ty = p.step("vector", (lambda e, x=xt, q=sq, s=st: e.tensor_scalar(
                q, x, s[:, 2:3], s[:, 6:7], ALU.subtract, ALU.mult)), [tr])
            xt_guard[t % 3] = [ty, t_sq]
            p.run("vector", (lambda e, q=sq, c=ca_rep: e.tensor_tensor(q, q, c, ALU.mult)), [t_ca])
            hw = [t_sa] + ([htm_guard[t % 2]] if (t % 2) in htm_guard else [])
            th = p.step("vector", (lambda e, q=sq, s=sa_rep, h=htm: e.tensor_tensor(h, q, s, ALU.add)), hw)
            sq_guard[t % 2] = th
            if consumer is not None:
                ev, rd = consumer(t, htm, th)
            else:
                ev, rd = self.tm_to_fm(htm, hfm, t, [th])
            htm_guard[t % 2] = rd
            ev_toks.append(ev)
        SC.free()
        XT.free()
        ca_buf.free()
        sa_buf.free()
        return HFM, ev_toks

    def tm_to_fm(self, src_tm, fm3d, t, waits, kc=KC, src_cols=None):
        p = self.p
        ptb = self.next_pt()
        gi = self.pt.index(ptb)
        w = list(waits) + [self.const_tok]
        if self.pt_guard[gi] is not None:
            w.append(self.pt_guard[gi])
        tok = None
        for k in range(kc):
            src = src_tm[:, k * 128:(k + 1) * 128] if src_cols is None else src_cols(k)
            tok = p.step("tensor", (lambda e, s=src, k=k, b=ptb:
                         e.transpose(b[:, k * 128:(k + 1) * 128], s, self.idn)),
                         w if k == 0 else ())
        ptv = ptb[:].rearrange("p (k q) -> p k q", k=8)[:, 0:kc, :]
        ev = p.step("vector", (lambda e, pv=ptv, o=fm3d[:, :, t * 128:(t + 1) * 128]:
                    e.tensor_copy(o, pv)), [tok])
        self.pt_guard[gi] = ev
        return ev, tok

    def l0_attn(self, HFM, hfm_toks):
        p = self.p
        hfm = HFM.view([128, KC, T], BF16)
        QKLN = self.al.alloc("QKLN", 64 * 1024)
        qkln = QKLN.view([128, NT, 2 * D], BF16)
        VTM = self.al.alloc("VTM", 32 * 1024)
        vtm = VTM.view([128, NT, D], BF16)
        SB = self.al.alloc("a0sb", 20 * 1024)
        srep = SB.view([128, 512], F32, 0)
        brep = SB.view([128, 512], F32, 2048)
        negB = SB.view([128, 1], F32, 8704)
        negrow = SB.view([1, 128], BF16, 9216)
        onesrow = SB.view([1, 128], BF16, 9472)
        WST = self.al.alloc("a0w", 2 * 8192)

        skip_aux = (self.upto == "l0proj")
        toks = list(self.mod_tok) + list(hfm_toks)
        for g in range(8 if not skip_aux else 0):
            nm = ("q_scale" if g % 2 == 0 else "k_scale")
            toks.append(p.dma("sync", SB.view([1, 512], F32, 0)[:, g * 64:(g + 1) * 64],
                              self.dram[nm].ap(), list(self.mod_tok) + list(hfm_toks)))
            nm = ("q_bias" if g % 2 == 0 else "k_bias")
            toks.append(p.dma("sync", SB.view([1, 512], F32, 2048)[:, g * 64:(g + 1) * 64],
                              self.dram[nm].ap()))
        trep = self.replicate_row(SB, SB.view([1, 1024], F32, 0), 4096, toks) \
            if not skip_aux else None
        tsc1 = None if skip_aux else p.step("vector", (lambda e: e.tensor_scalar(
            srep.rearrange("p (g two j) -> p g two j", g=4, two=2)[:, :, 0, :],
            srep.rearrange("p (g two j) -> p g two j", g=4, two=2)[:, :, 0, :],
            1.0 / 64, None, ALU.mult)), [trep])
        tsc2 = None if skip_aux else p.step("vector", (lambda e: e.tensor_scalar(
            brep.rearrange("p (g two j) -> p g two j", g=4, two=2)[:, :, 0, :],
            brep.rearrange("p (g two j) -> p g two j", g=4, two=2)[:, :, 0, :],
            1.0 / 64, None, ALU.mult)), [trep])
        sb_tok = [] if skip_aux else [tsc1, tsc2]

        tt0 = SB.view([1, 64], F32, 8768)
        tt1 = SB.view([1, 8], F32, 9056)
        nb_toks = []
        for j, (snm, bnm) in enumerate(() if skip_aux else (("q_scale", "q_bias"), ("k_scale", "k_bias"))):
            t1 = p.dma("sync", tt0, self.dram[snm].ap(), hfm_toks)
            tm = p.step("vector", (lambda e, j=j: _rmax(e, 
                tt1[:, 4 * j:4 * j + 1], tt0, axis=AX.X, apply_absolute_value=True)), [t1])
            t2 = p.dma("sync", tt0, self.dram[bnm].ap(), [tm])
            tq = p.step("vector", (lambda e: e.tensor_tensor(tt0, tt0, tt0, ALU.mult)), [t2])
            tsum = p.step("vector", (lambda e, j=j: _rsum(e, 
                tt1[:, 4 * j + 1:4 * j + 2], tt0, axis=AX.X)), [tq])
            tsq = p.step("scalar", (lambda e, j=j: e.activation(
                tt1[:, 4 * j + 2:4 * j + 3], tt1[:, 4 * j + 1:4 * j + 2], AF.Sqrt)), [tsum])
            nb_toks.append(p.step("vector", (lambda e, j=j: e.tensor_scalar(
                tt1[:, 4 * j:4 * j + 1], tt1[:, 4 * j:4 * j + 1], 8.004,
                None, ALU.mult)), [tm]))
            nb_toks.append(p.step("vector", (lambda e, j=j: e.tensor_tensor(
                tt1[:, 4 * j:4 * j + 1], tt1[:, 4 * j:4 * j + 1],
                tt1[:, 4 * j + 2:4 * j + 3], ALU.add)), [tsq]))
        if not skip_aux:
            tb = p.step("vector", (lambda e: e.tensor_tensor(
                tt1[:, 0:1], tt1[:, 0:1], tt1[:, 4:5], ALU.mult)), nb_toks)
            tb = p.step("vector", (lambda e: e.tensor_scalar(
                tt1[:, 0:1], tt1[:, 0:1], -1.004 / 64, None, ALU.mult)), [tb])
            negB_tok = self.replicate_row(SB, tt1[:, 0:1], 4, [tb], byte_off=8704)
        else:
            negB_tok = None

        mk_tok = p.step("gpsimd", (lambda e: (e.memset(negrow, NEG),
                                              e.memset(onesrow, 1.0))[-1]), hfm_toks)

        if self.upto == "l0pat":
            self.dump({"srep": srep, "brep": brep, "negB": negB},
                      [trep, tsc1, tsc2, negB_tok, mk_tok])
            return None
        wqk_src = self.dram["W_qk"].ap().rearrange("(k p) n -> p k n", p=128)
        wv_src = self.dram["W_v"].ap().rearrange("(k p) n -> p k n", p=128)
        qkln_tok = [[] for _ in range(NT)]
        vtm_tok = [[] for _ in range(NT)]
        wtok = [None, None]
        sq_guard = {}
        st_guard = {}
        stv_guard = {}
        for ng in range(4):
            wbuf = WST.view([128, KC, 512], BF16, (ng % 2) * 8192)
            ww = ([wtok[ng % 2]] if wtok[ng % 2] else []) + list(self.mod_tok) + list(hfm_toks)
            wtok[ng % 2] = p.dma("gpsimd", wbuf, wqk_src[:, :, ng * 512:(ng + 1) * 512], ww)
            last_mm = None
            for t in range(NT):
                bank = self.next_ps()
                gi = self.ps.index(bank)
                mmw = [wtok[ng % 2]] + hfm_toks + self.ps_guard[gi]
                for k in range(KC):
                    last_mm = p.step("tensor", (lambda e, b=bank, k=k, wb=wbuf, t=t,
                                     st=(k == 0), sp=(k == KC - 1):
                                     e.matmul(b[:], hfm[:, k, t * 128:(t + 1) * 128].opt(),
                                              wb[:, k, :].opt(), start=st, stop=sp)),
                                     mmw if k == 0 else ())
                if self.upto == "l0proj":
                    tqk = p.step("vector", (lambda e, b=bank, t=t, ng=ng: e.tensor_copy(
                        qkln[:, t, ng * 512:(ng + 1) * 512], b[:])), [last_mm])
                    self.ps_guard[gi] = [tqk]
                    qkln_tok[t].append(tqk)
                    continue
                i = (ng * NT + t)
                sq = SB.view([128, 512], F32, 4096 + (i % 2) * 2048)
                raw = SB.view([128, 512], F32, 12288 + (i % 2) * 2048)
                stv = SB.view([128, 64], F32, 8192 + (i % 2) * 256)
                sqw = [sq_guard[i % 2]] if (i % 2) in sq_guard else []
                stw = [st_guard[i % 2]] if (i % 2) in st_guard else []
                t_cp = p.step("vector", (lambda e, b=bank, r=raw: (
                    _vgap(e, 6), e.tensor_copy(r, b[:]))[-1]), [last_mm] + stw + sqw)
                self.ps_guard[gi] = [t_cp]
                pv = raw.rearrange("p (g j) -> p g j", g=8)
                p.run("vector", (lambda e, pv=pv, s=stv: _rsum(e, 
                    s[:, 0:8], pv, axis=AX.X)))
                tmn = p.step("vector", (lambda e, s=stv: e.tensor_scalar(
                    s[:, 16:24], s[:, 0:8], 1.0 / 64, 0.0, ALU.mult, ALU.add)))
                p.vspace(4)
                sqv3 = sq.rearrange("p (g j) -> p g j", g=8)
                mb = stv[:, 16:24].unsqueeze(2).broadcast_to([128, 8, 64])
                tsub = p.step("vector", (lambda e, q=sqv3, r=pv, mb=mb: e.tensor_tensor(
                    q, r, mb, ALU.subtract)), sqw)
                t_sq = p.step("vector", (lambda e, r=raw, q=sq: e.tensor_tensor(
                    r, q, q, ALU.mult)), [tsub])
                tvv = p.step("vector", (lambda e, pv=pv, s=stv: _rsum(e, 
                    s[:, 24:32], pv, axis=AX.X)),
                    [t_sq] + ([stv_guard[i % 2]] if (i % 2) in stv_guard else []))
                tvv = p.step("vector", (lambda e, s=stv: e.tensor_scalar(
                    s[:, 24:32], s[:, 24:32], 1.0 / 64, 0.0, ALU.mult, ALU.add)))
                tsd = p.step("scalar", (lambda e, s=stv: e.activation(
                    s[:, 40:48], s[:, 24:32], AF.Sqrt, bias=self.eps6)),
                    [tvv, self.const_tok])
                stv_guard[i % 2] = tsd
                trc = p.step("vector", (lambda e, s=stv: e.reciprocal(
                    s[:, 48:56], s[:, 40:48])), [tsd])
                p.vspace(4)
                rb = stv[:, 48:56].unsqueeze(2).broadcast_to([128, 8, 64])
                p.run("vector", (lambda e, q=sqv3, r=pv, rb=rb: e.tensor_tensor(
                    r, q, rb, ALU.mult)), [trc])
                p.run("vector", (lambda e, r=raw: e.tensor_tensor(
                    r, r, srep, ALU.mult)), sb_tok)
                tqk = p.step("vector", (lambda e, r=raw, t=t, ng=ng: e.tensor_tensor(
                    qkln[:, t, ng * 512:(ng + 1) * 512], r, brep, ALU.add)))
                sq_guard[i % 2] = t_sq
                st_guard[i % 2] = t_sq
                qkln_tok[t].append(tqk)
                if self.upto == "l0stat" and i == 0:
                    SNAP = self.al.alloc("snap", 8 * 1024)
                    tsn = p.step("vector", (lambda e, s=stv: e.tensor_copy(
                        SNAP.view([128, 64], F32, 0), s)), [tvv])
                    tsn2 = p.step("vector", (lambda e, q=sq: e.tensor_copy(
                        SNAP.view([128, 512], F32, 2048), q)), [tvv])
                    tsn3 = p.step("vector", (lambda e, r=raw: e.tensor_copy(
                        SNAP.view([128, 512], F32, 4096), r)), [tvv])
                    self.dump({"stv": SNAP.view([128, 64], F32, 0),
                               "sqs": SNAP.view([128, 512], F32, 2048),
                               "raws": SNAP.view([128, 512], F32, 4096)},
                              [tsn, tsn2, tsn3])
                    self._stopped = True
                    return None
            wtok[ng % 2] = last_mm
        for ng in range(2):
            wbuf = WST.view([128, KC, 512], BF16, (ng % 2) * 8192)
            ww = [wtok[ng % 2]] if wtok[ng % 2] else []
            wtok[ng % 2] = p.dma("gpsimd", wbuf, wv_src[:, :, ng * 512:(ng + 1) * 512], ww)
            for t in range(NT):
                bank = self.next_ps()
                gi = self.ps.index(bank)
                mmw = [wtok[ng % 2]] + hfm_toks + self.ps_guard[gi]
                last_mm = None
                for k in range(KC):
                    last_mm = p.step("tensor", (lambda e, b=bank, k=k, wb=wbuf, t=t,
                                     st=(k == 0), sp=(k == KC - 1):
                                     e.matmul(b[:], hfm[:, k, t * 128:(t + 1) * 128].opt(),
                                              wb[:, k, :].opt(), start=st, stop=sp)),
                                     mmw if k == 0 else ())
                tev = p.step("scalar", (lambda e, b=bank, t=t, ng=ng: e.activation(
                    vtm[:, t, ng * 512:(ng + 1) * 512], b[:], AF.Copy,
                    scale=1.0 / 32)), [last_mm])
                self.ps_guard[gi] = [tev]
                vtm_tok[t].append(tev)
            wtok[ng % 2] = last_mm

        if self.upto in ("l0qkln", "l0proj"):
            self.dump({"qkln": qkln.rearrange("p a b -> p (a b)"),
                       "vtm": vtm.rearrange("p a b -> p (a b)")},
                      [t for l in qkln_tok for t in l] + [t for l in vtm_tok for t in l])
            return None
        PH = self.al.alloc("a0ph", 4096 + 4096 + 2432 + 16 * 1024)
        QF = PH.view([64, T], BF16, 0)
        KF = PH.view([64, T], BF16, 4096)
        VA = PH.view([128, 16, 65], BF16, 8192)
        stat = SB.view([128, 8], F32, 9728)
        ptq_off = 8192 + 2432
        all_qk = [tok for lst in qkln_tok for tok in lst]
        all_v = [tok for lst in vtm_tok for tok in lst]
        head_rd_tok = []
        ptq_rd = {}
        ctx_tok = [[] for _ in range(NT)]
        exp_guard = {}
        for h in range(HEADS):
            tp_toks = []
            for part, dstbuf in ((0, QF), (1, KF)):
                for half in range(2):
                    ptb = self.next_pt()
                    gi = self.pt.index(ptb)
                    w = list(all_qk) + [self.const_tok] + head_rd_tok
                    if self.pt_guard[gi] is not None:
                        w.append(self.pt_guard[gi])
                    tok = None
                    for j in range(8):
                        t = half * 8 + j
                        src = qkln[:, t, 128 * h + 64 * part:128 * h + 64 * part + 64]
                        tok = p.step("tensor", (lambda e, s=src, j=j, b=ptb:
                                     e.transpose(b[0:64, j * 128:(j + 1) * 128], s,
                                                 self.idn)), w if j == 0 else ())
                    ev = p.step("vector", (lambda e, b=ptb, d=dstbuf, half=half:
                                e.tensor_copy(d[0:64, half * 1024:(half + 1) * 1024],
                                              b[0:64, :])), [tok])
                    self.pt_guard[gi] = ev
                    tp_toks.append(ev)
            va_w = all_v + head_rd_tok
            tva = p.step("vector", (lambda e, h=h: e.tensor_copy(
                VA[:, :, 0:64], vtm[:, :, h * 64:(h + 1) * 64])), va_w)
            tva2 = p.step("vector", (lambda e: e.memset(VA[:, :, 64:65], 1.0)), [tva])
            head_mm = []
            for qb in range(4):
                exp_toks = {}
                for kt in range(4 * qb + 4):
                    bank = self.next_ps()
                    gi = self.ps.index(bank)
                    mmw = tp_toks + self.ps_guard[gi] + [mk_tok]
                    masks = []
                    for j in range(4):
                        rel = (qb * 512 + j * 128) - kt * 128
                        if rel < 0:
                            masks.append((j, "neg"))
                        elif rel == 0:
                            masks.append((j, "tri"))
                    nops = 1 + len(masks)
                    tok = p.step("tensor", (lambda e, b=bank, kt=kt, qb=qb,
                                 sp=(nops == 1):
                                 e.matmul(b[:], KF[0:64, kt * 128:(kt + 1) * 128],
                                          QF[0:64, qb * 512:(qb + 1) * 512],
                                          start=True, stop=sp)), mmw)
                    for i, (j, kind) in enumerate(masks):
                        lhs, rhs = ((negrow, onesrow) if kind == "neg"
                                    else (self.tri, self.idn))
                        tok = p.step("tensor", (lambda e, b=bank, j=j, lhs=lhs, rhs=rhs,
                                     sp=(i == len(masks) - 1):
                                     e.matmul(b[:, j * 128:(j + 1) * 128], lhs,
                                              rhs, start=False, stop=sp)))
                    ptqv = PH.view([128, 512], BF16, ptq_off + kt * 1024)
                    ew = [tok, negB_tok] + ([ptq_rd[kt]] if kt in ptq_rd else []) \
                        + ([exp_guard[kt]] if kt in exp_guard else [])
                    te = p.step("scalar", (lambda e, b=bank, o=ptqv: e.activation(
                        o, b[:], AF.Exp, bias=negB)), ew)
                    self.ps_guard[gi] = [te]
                    exp_toks[kt] = te
                    exp_guard[kt] = te
                for qt in range(4):
                    qtg = qb * 4 + qt
                    bank = self.next_ps()
                    gi = self.ps.index(bank)
                    pvw = [exp_toks[kt] for kt in range(qtg + 1)] + [tva2] \
                        + self.ps_guard[gi]
                    tok = None
                    for kt in range(qtg + 1):
                        ptqv = PH.view([128, 512], BF16, ptq_off + kt * 1024)
                        tok = p.step("tensor", (lambda e, b=bank, pt_=ptqv, kt=kt, qt=qt,
                                     st=(kt == 0), sp=(kt == qtg):
                                     e.matmul(b[:, 0:65], pt_[:, qt * 128:(qt + 1) * 128],
                                              VA[:, kt, :].opt(), start=st, stop=sp)),
                                     pvw if kt == 0 else ())
                        ptq_rd[kt] = tok
                    trc = p.step("vector", (lambda e, b=bank: e.reciprocal(
                        stat[:, 0:1], b[:, 64:65])), [tok])
                    tcx = p.step("vector", (lambda e, b=bank, qtg=qtg, h=h: e.tensor_tensor(
                        qkln[:, qtg, 128 * h:128 * h + 64], b[:, 0:64],
                        stat[:, 0:1].broadcast_to([128, 64]), ALU.mult)), [trc])
                    self.ps_guard[gi] = [tcx]
                    ctx_tok[qtg].append(tcx)
            head_rd_tok = [tok]

        if self.upto == "l0ctx":
            self.dump({"ctx": qkln.rearrange("p a b -> p (a b)")},
                      [t for l in ctx_tok for t in l])
            return None
        PH.free()
        XT = self.al.alloc("a0xt", 3 * 4096 + 2 * 2048 + 2 * 2048)
        ctxf = hfm
        ctxf_toks = []
        ctmp_g = {}
        for t in range(NT):
            ctmp = XT.view([128, D], BF16, 3 * 4096 + 2 * 2048 + (t % 2) * 2048)
            cw = ctx_tok[t] + head_rd_tok + ([ctmp_g[t % 2]] if (t % 2) in ctmp_g else [])
            tcc = p.step("vector", (lambda e, t=t, c=ctmp: e.tensor_copy(
                c.rearrange("p (hh c) -> p hh c", hh=16),
                qkln[:, t, :].rearrange("p (hh c) -> p hh c", hh=16)[:, :, 0:64])), cw)
            ev, rd = self.tm_to_fm(ctmp, ctxf, t, [tcc])
            ctmp_g[t % 2] = rd
            ctxf_toks.append(ev)
        wo_src = self.dram["W_out"].ap().rearrange("(k p) n -> p k n", p=128)
        wo_tok = []
        for ng in range(2):
            wbuf = WST.view([128, KC, 512], BF16, ng * 8192)
            wo_tok.append(p.dma("gpsimd", wbuf,
                                wo_src[:, :, ng * 512:(ng + 1) * 512], [wtok[ng]]))
        ga_buf, ga_rep, ga_tok = self.mod_rep(2, extra_waits=ctxf_toks)
        xsrc = self.dram[self.x_src].ap().rearrange("(t p) d -> p t d", p=128)
        xdst = self.dram["xres"].ap().rearrange("(t p) d -> p t d", p=128)
        xg = {}
        rtmp_g = {}
        for t in range(NT):
            xt = XT.view([128, D], F32, (t % 3) * 4096)
            xw = [w for w in [self.x_tile_tok[t], xg.get(t % 3)] if w] + ctxf_toks
            xld = p.dma("sync", xt, xsrc[:, t, :], xw)
            tadd = None
            for ng in range(2):
                bank = self.next_ps()
                gi = self.ps.index(bank)
                wbuf = WST.view([128, KC, 512], BF16, ng * 8192)
                mmw = [wo_tok[ng]] + ctxf_toks + self.ps_guard[gi]
                last_mm = None
                for k in range(KC):
                    last_mm = p.step("tensor", (lambda e, b=bank, k=k, wb=wbuf, t=t:
                                     e.matmul(b[:], ctxf[:, k, t * 128:(t + 1) * 128].opt(),
                                              wb[:, k, :].opt(), start=(k == 0),
                                              stop=(k == KC - 1))),
                                     mmw if k == 0 else ())
                rtmp = XT.view([128, 512], F32, 3 * 4096 + ng * 2048)
                rw = [last_mm, ga_tok] + ([rtmp_g[ng]] if ng in rtmp_g else [])
                tmul = p.step("vector", (lambda e, b=bank, r=rtmp, ng=ng:
                              e.scalar_tensor_tensor(r, b[:], 1.0 / 32,
                                                     ga_rep[:, ng * 512:(ng + 1) * 512],
                                                     ALU.mult, ALU.mult)), rw)
                self.ps_guard[gi] = [tmul]
                tadd = p.step("vector", (lambda e, x=xt, r=rtmp, ng=ng: e.tensor_tensor(
                    x[:, ng * 512:(ng + 1) * 512], x[:, ng * 512:(ng + 1) * 512],
                    r, ALU.add)), [xld, tmul])
                rtmp_g[ng] = tadd
            tst = p.dma("sync", xdst[:, t, :], xt, [tadd])
            xg[t % 3] = tst
            self.x_tile_tok[t] = tst
        self.x_src = "xres"
        for b in (QKLN, VTM, SB, WST, XT, ga_buf, HFM):
            b.free()
        return None

    def l0_ffn(self, HFM, hfm_toks):
        p = self.p
        hfm = HFM.view([128, KC, T], BF16)
        SB = self.al.alloc("f0sb", 24 * 1024)
        onesrow = SB.view([1, 128], BF16, 0)
        mk_tok = p.step("gpsimd", (lambda e: e.memset(onesrow, 1.0)), hfm_toks)
        WP = self.al.alloc("f0w", 2 * 8192)
        bufs = {}
        out_toks = {}
        wtok = [None, None]
        for fi, (pre, kind) in enumerate((("fq", "qfm"), ("fk", "kfm"), ("fv", "vtm"))):
            nwhi = SB.view([15, D], BF16, 256)
            nwlo = SB.view([15, D], BF16, 2304)
            nwf = SB.view([15, D], F32, 4352)
            nwthi = SB.view([128, KC, 16], BF16, 21504)
            nwtlo = SB.view([128, KC, 16], BF16, 21504 + 256)
            b1r = SB.view([1, D], BF16, 8448)
            b2v = SB.view([16, D], BF16, 10496)
            selfm = SB.view([16, T], BF16, 12544)
            nbr = SB.view([1, 15], F32, 16640)
            sc15 = SB.view([128, NT, 15], F32, 16704)
            ps4 = SB.view([128, 64], F32, 18432)
            seltm = SB.view([128, NT, 16], BF16, 18688)
            scr = SB.view([128, 512], F32, 20480)
            w = list(self.mod_tok) + hfm_toks + (out_toks.get(fi - 1) or [])
            tnw = p.dma("gpsimd", nwhi, self.dram[pre + "_nw"].ap(), w)
            tnwf = p.dma("sync", nwf, self.dram[pre + "_nw"].ap(), w)
            tlo = p.step("vector", (lambda e, a=nwlo, b=nwf, c=nwhi: e.tensor_tensor(
                a, b, c, ALU.subtract)), [tnw, tnwf])
            tb1 = p.dma("gpsimd", b1r, self.dram[pre + "_b1"].ap(), w)
            tb2 = p.dma("gpsimd", b2v, self.dram[pre + "_b2"].ap(), w)
            tnb = p.dma("sync", nbr, self.dram[pre + "_nb"].ap(), w)
            tnb = self.replicate_row(SB, nbr, 64, [tnb], byte_off=16640, have_row0=True)
            nbrep = SB.view([128, 15], F32, 16640)
            nwt_toks = []
            for hl, (nv, dv) in enumerate(((nwhi, nwthi), (nwlo, nwtlo))):
                ptb = self.next_pt()
                gi2 = self.pt.index(ptb)
                w4 = [tnw, tlo, self.const_tok] + ([self.pt_guard[gi2]] if self.pt_guard[gi2] else [])
                tok = None
                for k in range(KC):
                    tok = p.step("tensor", (lambda e, k=k, nv=nv, b=ptb:
                                 e.transpose(b[:, k * 16:k * 16 + 15],
                                             nv[0:15, k * 128:(k + 1) * 128].opt(),
                                             self.idn[0:15, 0:15])), w4 if k == 0 else ())
                ev = p.step("vector", (lambda e, b=ptb, dv=dv: e.tensor_copy(
                    dv.rearrange("p a b -> p (a b)")[:, 0:128],
                    b[:, 0:128])), [tok])
                self.pt_guard[gi2] = ev
                nwt_toks.append(ev)
            sel_toks = []
            for t in range(NT):
                bank = self.next_ps()
                gi = self.ps.index(bank)
                mw = nwt_toks + self.ps_guard[gi] + hfm_toks
                tok = None
                for hl, nwv in ((0, nwthi), (1, nwtlo)):
                    for k in range(KC):
                        tok = p.step("tensor", (lambda e, b=bank, k=k, t=t, nv=nwv,
                                     st=(hl == 0 and k == 0), sp=(hl == 1 and k == KC - 1):
                                     e.matmul(b[:, 0:15], hfm[:, k, t * 128:(t + 1) * 128].opt(),
                                              nv[:, k, 0:15].opt(),
                                              start=st, stop=sp)), mw if (hl == 0 and k == 0) else ())
                sct = sc15[:, t, :]
                tev = p.step("vector", (lambda e, b=bank, s=sct: e.tensor_copy(
                    s, b[:, 0:15])), [tok])
                self.ps_guard[gi] = [tev]
                tnbad = p.step("vector", (lambda e, s=sct: e.tensor_tensor(
                    s, s, nbrep, ALU.add)), [tnb])
                p.vspace(4)
                segs = [(0, 1), (1, 2), (3, 4), (7, 8)]
                ptok = None
                for lvl in range(4):
                    base, cnt = segs[lvl]
                    g_ = ps4[:, 32:32 + cnt]
                    gc = ps4[:, 48:48 + cnt]
                    p.run("vector", (lambda e, s=sct, g=g_, b=base, c=cnt: e.tensor_scalar(
                        g, s[:, b:b + c], 0.0, None, ALU.is_ge)))
                    p.run("vector", (lambda e, s=sct, g=gc, b=base, c=cnt: e.tensor_scalar(
                        g, s[:, b:b + c], 0.0, None, ALU.is_lt)))
                    p.vspace(4)
                    pdst = ps4[:, 16:16 + 2 * cnt].rearrange("p (a two) -> p a two", two=2)
                    if lvl == 0:
                        p.run("vector", (lambda e, g=gc, d=pdst: e.tensor_copy(d[:, :, 0], g)))
                        ptok = p.step("vector", (lambda e, g=g_, d=pdst: e.tensor_copy(d[:, :, 1], g)))
                    else:
                        psrc = ps4[:, 0:cnt]
                        p.run("vector", (lambda e, s=psrc, g=gc, d=pdst: e.tensor_tensor(
                            d[:, :, 0], s, g, ALU.mult)))
                        ptok = p.step("vector", (lambda e, s=psrc, g=g_, d=pdst: e.tensor_tensor(
                            d[:, :, 1], s, g, ALU.mult)))
                    p.vspace(4)
                    if lvl < 3:
                        ptok = p.step("vector", (lambda e, c=cnt, d=ps4: e.tensor_copy(
                            d[:, 0:2 * c], d[:, 16:16 + 2 * c])))
                        p.vspace(4)
                tsel = p.step("vector", (lambda e, t=t: e.tensor_copy(
                    seltm[:, t, :], ps4[:, 16:32])), [ptok])
                sel_toks.append(tsel)
            selfm_toks = []
            for half in range(2):
                ptb = self.next_pt()
                gi2 = self.pt.index(ptb)
                w2_ = sel_toks + [self.const_tok]
                if self.pt_guard[gi2] is not None:
                    w2_.append(self.pt_guard[gi2])
                tok = None
                for j in range(8):
                    t = half * 8 + j
                    tok = p.step("tensor", (lambda e, t=t, j=j, b=ptb:
                                 e.transpose(b[0:16, j * 128:(j + 1) * 128],
                                             seltm[:, t, :].opt(), self.idn)),
                                 w2_ if j == 0 else ())
                ev = p.step("vector", (lambda e, b=ptb, half=half: e.tensor_copy(
                    selfm[0:16, half * 1024:(half + 1) * 1024], b[0:16, :])), [tok])
                self.pt_guard[gi2] = ev
                selfm_toks.append(ev)
            HIDF = self.al.alloc("f0hid", 32 * 1024)
            hidf = HIDF.view([128, KC, T], BF16)
            w1src = self.dram[pre + "_w1"].ap().rearrange("(k p) n -> p k n", p=128)
            hid_toks = []
            for ng in range(2):
                wb = WP.view([128, KC, 512], BF16, ng * 8192)
                ww = ([wtok[ng]] if wtok[ng] else []) + hfm_toks + [mk_tok]
                wtok[ng] = p.dma("gpsimd", wb, w1src[:, :, ng * 512:(ng + 1) * 512], ww)
            for t in range(NT):
                htile = SB.view([128, D], BF16, 22528)
                for ng in range(2):
                    wb = WP.view([128, KC, 512], BF16, ng * 8192)
                    bank = self.next_ps()
                    gi = self.ps.index(bank)
                    mw = [wtok[ng], tb1, mk_tok] + self.ps_guard[gi]
                    tok = None
                    for k in range(KC):
                        tok = p.step("tensor", (lambda e, b=bank, k=k, t=t, wb=wb:
                                     e.matmul(b[:], hfm[:, k, t * 128:(t + 1) * 128].opt(),
                                              wb[:, k, :].opt(), start=(k == 0),
                                              stop=False)), mw if k == 0 else ())
                    tok = p.step("tensor", (lambda e, b=bank, ng=ng:
                                 e.matmul(b[:], onesrow, b1r[:, ng * 512:(ng + 1) * 512],
                                          start=False, stop=True)))
                    trl = p.step("scalar", (lambda e, b=bank, h=htile, ng=ng: e.activation(
                        h[:, ng * 512:(ng + 1) * 512], b[:], AF.Relu)), [tok])
                    self.ps_guard[gi] = [trl]
                selb = seltm[:, t, :].unsqueeze(2).broadcast_to([128, 16, 64])
                htile3 = SB.view([128, 16, 64], BF16, 22528)
                tmk = p.step("vector", (lambda e, h=htile3, sb=selb: e.tensor_tensor(
                    h, h, sb, ALU.mult)), [trl, sel_toks[t]])
                ev, _ = self.tm_to_fm(htile, hidf, t, [tmk])
                hid_toks.append(ev)
            w2src = self.dram[pre + "_w2"].ap().rearrange("(k p) n -> p k n", p=128)
            if kind == "vtm":
                HFM.free()
            bufs[kind] = self.al.alloc("f0" + kind, 32 * 1024)
            if kind in ("qfm", "kfm"):
                dstf = bufs[kind].view([128, KC, T], BF16)
                otoks = []
                for ng in range(2):
                    wb = WP.view([128, KC, 512], BF16, ng * 8192)
                    ww = [wtok[ng]] + hid_toks
                    wtok[ng] = p.dma("gpsimd", wb, w2src[:, :, ng * 512:(ng + 1) * 512], ww)
                for oc in range(KC):
                    ng, off = divmod(oc * 128, 512)
                    wb = WP.view([128, KC, 512], BF16, ng * 8192)
                    for ts_ in range(4):
                        bank = self.next_ps()
                        gi = self.ps.index(bank)
                        mw = [wtok[ng], tb2] + hid_toks + selfm_toks + self.ps_guard[gi]
                        tok = None
                        for k in range(KC):
                            tok = p.step("tensor", (lambda e, b=bank, k=k, oc=oc, off=off,
                                         ts_=ts_, wb=wb:
                                         e.matmul(b[:], wb[:, k, off:off + 128].opt(),
                                                  hidf[:, k, ts_ * 512:(ts_ + 1) * 512].opt(),
                                                  start=(k == 0), stop=False)),
                                         mw if k == 0 else ())
                        tok = p.step("tensor", (lambda e, b=bank, oc=oc, ts_=ts_:
                                     e.matmul(b[:], b2v[0:16, oc * 128:(oc + 1) * 128].opt(),
                                              selfm[0:16, ts_ * 512:(ts_ + 1) * 512].opt(),
                                              start=False, stop=True)))
                        tev = p.step("vector", (lambda e, b=bank, oc=oc, ts_=ts_, d=dstf:
                                     e.tensor_copy(d[:, oc, ts_ * 512:(ts_ + 1) * 512], b[:])),
                                     [tok])
                        self.ps_guard[gi] = [tev]
                        otoks.append(tev)
                out_toks[fi] = otoks
            else:
                vt = bufs["vtm"].view([128, NT, D], BF16)
                otoks = []
                for ng in range(2):
                    wb = WP.view([128, KC, 512], BF16, ng * 8192)
                    ww = [wtok[ng]] + hid_toks
                    wtok[ng] = p.dma("gpsimd", wb, w2src[:, :, ng * 512:(ng + 1) * 512], ww)
                for t in range(NT):
                    for ng in range(2):
                        wb = WP.view([128, KC, 512], BF16, ng * 8192)
                        bank = self.next_ps()
                        gi = self.ps.index(bank)
                        mw = [wtok[ng], tb2] + hid_toks + selfm_toks + self.ps_guard[gi]
                        tok = None
                        for k in range(KC):
                            tok = p.step("tensor", (lambda e, b=bank, k=k, t=t, wb=wb:
                                         e.matmul(b[:], hidf[:, k, t * 128:(t + 1) * 128].opt(),
                                                  wb[:, k, :].opt(), start=(k == 0),
                                                  stop=False)), mw if k == 0 else ())
                        tok = p.step("tensor", (lambda e, b=bank, t=t, ng=ng:
                                     e.matmul(b[:], selfm[0:16, t * 128:(t + 1) * 128].opt(),
                                              b2v[0:16, ng * 512:(ng + 1) * 512].opt(),
                                              start=False, stop=True)))
                        tev = p.step("vector", (lambda e, b=bank, t=t, ng=ng:
                                     e.tensor_copy(vt[:, t, ng * 512:(ng + 1) * 512], b[:])),
                                     [tok])
                        self.ps_guard[gi] = [tev]
                        otoks.append(tev)
                out_toks[fi] = otoks
            HIDF.free()
        qf = bufs["qfm"].view([128, KC, T], BF16)
        kf = bufs["kfm"].view([128, KC, T], BF16)
        vt = bufs["vtm"].view([128, NT, D], BF16)
        SB.free()
        WP.free()
        FOUT = self.al.alloc("f0o", 32 * 1024)
        fout = FOUT.view([128, NT, D], BF16)
        AT = self.al.alloc("f0at", 17 * 1024)
        srow = AT.view([128, T], F32, 0)
        pq = AT.view([128, T], BF16, 8192)
        ptk = AT.view([128, 16, 128], BF16, 12288)
        stt = AT.view([128, 8], F32, 16384)
        fo_toks = []
        for qt in range(NT):
            ev_toks2 = []
            for kc4 in range(4):
                bank = self.next_ps()
                gi = self.ps.index(bank)
                mw = out_toks[0] + out_toks[1] + self.ps_guard[gi]
                tok = None
                for k in range(KC):
                    tok = p.step("tensor", (lambda e, b=bank, k=k, qt=qt, kc4=kc4:
                                 e.matmul(b[:], qf[:, k, qt * 128:(qt + 1) * 128].opt(),
                                          kf[:, k, kc4 * 512:(kc4 + 1) * 512].opt(),
                                          start=(k == 0), stop=(k == KC - 1))),
                                 mw if k == 0 else ())
                tev = p.step("vector", (lambda e, b=bank, kc4=kc4: e.tensor_copy(
                    srow[:, kc4 * 512:(kc4 + 1) * 512], b[:])), [tok])
                self.ps_guard[gi] = [tev]
                ev_toks2.append(tev)
            tmx = p.step("vector", (lambda e: _rmax(e, stt[:, 0:1], srow, axis=AX.X,
                                                    negate=True)), ev_toks2)
            tmx = p.step("vector", (lambda e: e.tensor_scalar(
                stt[:, 1:2], stt[:, 0:1], 1.0 / 32, 0.0, ALU.mult, ALU.add)), [tmx])
            p.vspace(4)
            texp = p.step("scalar", (lambda e: e.activation(
                pq, srow, AF.Exp, bias=stt[:, 1:2], scale=1.0 / 32)), [tmx])
            tsum = p.step("vector", (lambda e: _rsum(e, stt[:, 2:3], pq, axis=AX.X)), [texp])
            trc = p.step("vector", (lambda e: e.reciprocal(stt[:, 3:4], stt[:, 2:3])), [tsum])
            p.vspace(4)
            pt_toks = []
            for half in range(2):
                ptb = self.next_pt()
                gi2 = self.pt.index(ptb)
                w3 = [texp] + ([self.pt_guard[gi2]] if self.pt_guard[gi2] else [])
                tok = None
                for j in range(8):
                    kt = half * 8 + j
                    tok = p.step("tensor", (lambda e, kt=kt, j=j, b=ptb:
                                 e.transpose(b[:, j * 128:(j + 1) * 128],
                                             pq[:, kt * 128:(kt + 1) * 128], self.idn)),
                                 w3 if j == 0 else ())
                ev = p.step("vector", (lambda e, b=ptb, half=half: e.tensor_copy(
                    ptk[:, half * 8:(half + 1) * 8, :], b[:].rearrange(
                        "p (a q) -> p a q", a=8))), [tok])
                self.pt_guard[gi2] = ev
                pt_toks.append(ev)
            for ng in range(2):
                bank = self.next_ps()
                gi = self.ps.index(bank)
                mw = pt_toks + out_toks[2] + self.ps_guard[gi]
                tok = None
                for kt in range(NT):
                    tok = p.step("tensor", (lambda e, b=bank, kt=kt, ng=ng, qt=qt:
                                 e.matmul(b[:], ptk[:, kt, :].opt(),
                                          vt[:, kt, ng * 512:(ng + 1) * 512].opt(),
                                          start=(kt == 0), stop=(kt == NT - 1))),
                                 mw if kt == 0 else ())
                tev = p.step("vector", (lambda e, b=bank, qt=qt, ng=ng: e.tensor_tensor(
                    fout[:, qt, ng * 512:(ng + 1) * 512], b[:],
                    stt[:, 3:4].broadcast_to([128, 512]), ALU.mult)), [tok, trc])
                self.ps_guard[gi] = [tev]
                fo_toks.append(tev)
        gb_buf, gb_rep, gb_tok = self.mod_rep(5, extra_waits=fo_toks)
        xsrc = self.dram[self.x_src].ap().rearrange("(t p) d -> p t d", p=128)
        xdst = self.dram["xres"].ap().rearrange("(t p) d -> p t d", p=128)
        XT = self.al.alloc("f0xt", 3 * 4096 + 4096)
        xg = {}
        rg = {}
        for t in range(NT):
            xt = XT.view([128, D], F32, (t % 3) * 4096)
            xw = [self.x_tile_tok[t]] + [w for w in [xg.get(t % 3)] if w] + fo_toks
            xld = p.dma("sync", xt, xsrc[:, t, :], xw)
            rtmp = XT.view([128, D], F32, 3 * 4096)
            rw = [gb_tok] + ([rg.get(0)] if rg.get(0) else [])
            tm1 = p.step("vector", (lambda e, r=rtmp, t=t: e.tensor_tensor(
                r, fout[:, t, :], gb_rep, ALU.mult)), rw)
            tadd = p.step("vector", (lambda e, x=xt, r=rtmp: e.tensor_tensor(
                x, x, r, ALU.add)), [xld, tm1])
            rg[0] = tadd
            tst = p.dma("sync", xdst[:, t, :], xt, [tadd])
            xg[t % 3] = tst
            self.x_tile_tok[t] = tst
        for b in (bufs["qfm"], bufs["kfm"], bufs["vtm"], FOUT, AT, XT, gb_buf):
            b.free()

    def l1_attn(self, HFM, hfm_toks):
        p = self.p
        hfm = HFM.view([128, KC, T], BF16)
        U = 38
        NEGB = -9.0e5
        NBIS = 48
        QTM = self.al.alloc("a1q", 32 * 1024)
        KTM = self.al.alloc("a1k", 32 * 1024)
        VTM = self.al.alloc("a1v", 32 * 1024)
        SB = self.al.alloc("a1sb", 24 * 1024 + 512)
        WST = self.al.alloc("a1w", 2 * 8192)
        IDXR = self.al.alloc("a1idx", 12 * 1024)
        KA = self.al.alloc("a1ka", 4224)
        QA = self.al.alloc("a1qa", 4096)
        VAB = self.al.alloc("a1va", 2112)
        qtm = QTM.view([128, NT, D], BF16)
        ktm = KTM.view([128, NT, D], BF16)
        vtm = VTM.view([128, NT, D], BF16)
        brep = SB.view([128, D], F32, 0)
        iotac = SB.view([128, 1], F32, 4096)
        cntb = SB.view([128, 16], F32, 4160)
        wcol = SB.view([128, 16], BF16, 4224)
        b2tm = SB.view([128, 16], F32, 4288)
        IOTA38 = SB.view([128, 38], F32, 4352)
        stat = SB.view([128, 16], F32, 4544)
        Mtm = SB.view([128, 256], F32, 4608)
        M3 = SB.view([128, 16, 16], F32, 4608)
        lohi = SB.view([128, 96], F32, 5632)
        onescol = SB.view([128, 1], BF16, 6016)
        onesrow = SB.view([1, 128], BF16, 6080)
        c38r = SB.view([1, 16], BF16, 6336)
        scan = SB.view([1, 96], F32, 6400)
        exrow = SB.view([1, 16], BF16, 6784)
        UT = SB.view([128, 128], BF16, 6848)
        GEB3 = SB.view([128, 16, 16], BF16, 7104)
        prt = SB.view([128, 16], BF16, 7616)
        selF = SB.view([128, 16], F32, 7680)
        selh = SB.view([128, 16], BF16, 7744)
        rtm = SB.view([128, 16], F32, 7808)
        invq = SB.view([128, 16], F32, 7872)
        SELALL = SB.view([128, 256], BF16, 7936)
        ohT3 = SB.view([128, 16, 38], BF16, 8448)
        pt2 = SB.view([128, 16, 38], BF16, 9728)
        qrs = SB.view([38, HD], BF16, 11008)
        qrf = SB.view([64, 38], BF16, 11136)
        stt2 = SB.view([38, 8], F32, 11264)
        ctxn = SB.view([38, HD], BF16, 11328)
        vmb = SB.view([64, 1], BF16, 11456)
        vrow = SB.view([1, HD], BF16, 11520)
        VMR = SB.view([128, HD], F32, 11648)
        vtmp = SB.view([128, HD], F32, 11904)
        hirow = SB.view([1, 16], BF16, 12160)
        wclo = SB.view([128, 16], BF16, 24512)
        wcf = SB.view([128, 16], F32, 24576)
        kwt = SB.view([64, 1], F32, 24640)
        srow2 = SB.view([38, T], F32, 12224)
        pq2 = SB.view([38, T], BF16, 20416)
        ka = KA.view([65, 2064], BF16)
        qa = QA.view([65, T], BF16)
        va = VAB.view([128, 16, 65], BF16)
        lo = lohi[:, 0:16]
        hi = lohi[:, 16:32]
        mid = lohi[:, 32:48]
        t4 = lohi[:, 48:64]
        t5 = lohi[:, 64:80]
        prtf = lohi[:, 80:96]
        mk = p.step("gpsimd", (lambda e: (
            e.memset(onescol, 1.0), e.memset(onesrow, 1.0),
            e.memset(qa[64:65, 0:T], 1.0), e.memset(ka[64:65, 2048:2064], 0.0),
            e.memset(UT, 0.0),
            e.iota(IOTA38, pattern=[[1, 38]], base=1, channel_multiplier=0,
                   allow_small_or_imprecise_dtypes=True),
            e.iota(iotac, pattern=[[0, 1]], base=0, channel_multiplier=1,
                   allow_small_or_imprecise_dtypes=True),
            e.affine_select(out=UT, in_=UT, compare_op=ALU.is_ge, fill=1.0,
                            base=-1, pattern=[[-1, 128]], channel_multiplier=1),
        )[-1]), hfm_toks)

        wsrcs = {"q": "Wq", "k": "Wk", "v": "Wv"}
        bsrcs = {"q": "bq", "k": "bk", "v": "bv"}
        dsts = {"q": qtm, "k": ktm, "v": vtm}
        ptoks = {"q": [], "k": [], "v": []}
        wtok = [None, None]
        brep_rd = []
        for nm in ("q", "k", "v"):
            wsrc = self.dram[wsrcs[nm]].ap().rearrange("(k p) n -> p k n", p=128)
            tb = p.dma("sync", SB.view([1, D], F32, 0), self.dram[bsrcs[nm]].ap(),
                       hfm_toks + brep_rd + [mk])
            tb = self.replicate_row(SB, None, 4096, [tb], byte_off=0, have_row0=True)
            brep_rd = []
            for ng in range(2):
                wb = WST.view([128, KC, 512], BF16, ng * 8192)
                ww = ([wtok[ng]] if wtok[ng] else []) + hfm_toks + [mk]
                wtok[ng] = p.dma("gpsimd", wb, wsrc[:, :, ng * 512:(ng + 1) * 512], ww)
                tok = None
                for t in range(NT):
                    bank = self.next_ps()
                    gi = self.ps.index(bank)
                    mw = [wtok[ng], tb] + hfm_toks + self.ps_guard[gi]
                    for k in range(KC):
                        tok = p.step("tensor", (lambda e, b=bank, k=k, t=t, wb=wb:
                                     e.matmul(b[:], hfm[:, k, t * 128:(t + 1) * 128].opt(),
                                              wb[:, k, :].opt(), start=(k == 0),
                                              stop=(k == KC - 1))), mw if k == 0 else ())
                    tev = p.step("vector", (lambda e, b=bank, t=t, ng=ng, d=dsts[nm]:
                                 e.tensor_tensor(d[:, t, ng * 512:(ng + 1) * 512], b[:],
                                                 brep[:, ng * 512:(ng + 1) * 512],
                                                 ALU.add)), [tok, tb])
                    self.ps_guard[gi] = [tev]
                    ptoks[nm].append(tev)
                    brep_rd.append(tev)
                wtok[ng] = tok
        allq = ptoks["q"]
        allk = ptoks["k"]
        allv = ptoks["v"]
        HFM.free()

        idxi = IDXR.view([1, T], mybir.dt.int32, 0)
        idxf0 = IDXR.view([1, T], F32, 0)
        idxrep = IDXR.view([128, T], F32, 0)
        tix = p.dma("sync", idxi, self.dram["sample_idx"].ap(), hfm_toks + allq)
        tix = p.step("vector", (lambda e: e.tensor_copy(idxf0, idxi)), [tix])
        tix = self.replicate_row(IDXR, None, 8192, [tix], have_row0=True)
        EQ = SB.view([128, T], BF16, 12224)
        cnt_toks = []
        for kc in range(16):
            tbase = p.step("vector", (lambda e, kc=kc: e.tensor_scalar(
                stat[:, 0:1], iotac, float(kc * 128), 0.0, ALU.add, ALU.add)),
                [mk, tix])
            p.vspace(4)
            teq = p.step("vector", (lambda e: e.tensor_scalar(
                EQ, idxrep, stat[:, 0:1], None, ALU.is_equal)), [tbase])
            tc = p.step("vector", (lambda e, kc=kc: _rsum(
                e, cntb[:, kc:kc + 1], EQ, axis=AX.X)), [teq])
            cnt_toks.append(tc)
        tw = p.step("vector", (lambda e: e.tensor_scalar(
            wcf, cntb, 1.0 / T, None, ALU.mult)), cnt_toks)
        p.vspace(2)
        tw = p.step("vector", (lambda e: e.tensor_copy(wcol, wcf)))
        p.vspace(2)
        tw = p.step("vector", (lambda e: e.tensor_tensor(
            wcf, wcf, wcol, ALU.subtract)))
        p.vspace(2)
        tw = p.step("vector", (lambda e: e.tensor_copy(wclo, wcf)))
        p.vspace(2)
        tb2 = p.step("vector", (lambda e: e.tensor_scalar(
            b2tm, cntb, 0.5, None, ALU.is_lt)), cnt_toks)
        p.vspace(2)
        tb2 = p.step("vector", (lambda e: e.tensor_scalar(
            b2tm, b2tm, NEGB, 0.0, ALU.mult, ALU.add)), [tb2])
        tbr = None
        for kc in range(16):
            tbr = p.dma("gpsimd", ka[64:65, kc * 128:(kc + 1) * 128],
                        b2tm[:, kc:kc + 1], [tb2, mk], slow=True)
        IDXR.free()

        SRW = self.al.alloc("a1srw", 16 * 1024)
        srA = SRW.view([128, T], F32, 0)
        srB = SRW.view([128, T], F32, 8192)
        sr_guard = {}
        head_guard = []
        M_toks = []
        last_mm = None
        for h in range(HEADS):
            tfm = []
            for (src, dst, toks) in ((qtm, qa, allq), (ktm, ka, allk)):
                for half in range(2):
                    ptb = self.next_pt()
                    gi2 = self.pt.index(ptb)
                    w_ = toks + [self.const_tok, mk] + head_guard + \
                        ([self.pt_guard[gi2]] if self.pt_guard[gi2] else [])
                    tok = None
                    for j in range(8):
                        t = half * 8 + j
                        tok = p.step("tensor", (lambda e, t=t, j=j, b=ptb, src=src, h=h:
                                     e.transpose(b[0:64, j * 128:(j + 1) * 128],
                                                 src[:, t, h * 64:(h + 1) * 64],
                                                 self.idn)), w_ if j == 0 else ())
                    ev = p.step("vector", (lambda e, b=ptb, half=half, dst=dst:
                                e.tensor_copy(dst[0:64, half * 1024:(half + 1) * 1024],
                                              b[0:64, :])), [tok])
                    self.pt_guard[gi2] = ev
                    tfm.append(ev)
            tsc = p.step("vector", (lambda e: e.tensor_scalar(
                qa[0:64, 0:T], qa[0:64, 0:T], 0.125, None, ALU.mult)), tfm)
            bank = self.next_ps()
            gi = self.ps.index(bank)
            mw = allk + [tw] + self.ps_guard[gi] + tfm
            tok = None
            for ti, wv in enumerate((wcol, wclo)):
                for kc in range(16):
                    tok = p.step("tensor", (lambda e, b=bank, kc=kc, h=h, wv=wv,
                                 st=(ti == 0 and kc == 0), sp=(ti == 1 and kc == 15):
                                 e.matmul(b[0:64, 0:1], ktm[:, kc, h * 64:(h + 1) * 64].opt(),
                                          wv[:, kc:kc + 1], start=st, stop=sp)),
                                 mw if (ti == 0 and kc == 0) else ())
            tkw = p.step("vector", (lambda e, b=bank: e.tensor_copy(
                ka[0:64, 2048:2049], b[0:64, 0:1])), [tok])
            p.vspace(2)
            tkw = p.step("vector", (lambda e, b=bank: e.tensor_tensor(
                kwt, b[0:64, 0:1], ka[0:64, 2048:2049], ALU.subtract)))
            p.vspace(2)
            tkw = p.step("vector", (lambda e: e.tensor_copy(
                ka[0:64, 2049:2050], kwt)))
            p.vspace(2)
            self.ps_guard[gi] = [tkw]
            for qt in range(NT):
                i = h * 16 + qt
                sr = srA if i % 2 == 0 else srB
                bank = self.next_ps()
                gi = self.ps.index(bank)
                tok = p.step("tensor", (lambda e, b=bank, qt=qt:
                             e.matmul(b[:, 0:2], qa[0:65, qt * 128:(qt + 1) * 128],
                                      ka[0:65, 2048:2050], start=True, stop=True)),
                             self.ps_guard[gi] + [tkw, tsc, mk, tbr])
                tmc = p.step("vector", (lambda e, b=bank: e.tensor_copy(
                    stat[:, 5:7], b[:, 0:2])), [tok])
                p.vspace(2)
                tmc = p.step("vector", (lambda e: e.tensor_tensor(
                    stat[:, 5:6], stat[:, 5:6], stat[:, 6:7], ALU.add)))
                self.ps_guard[gi] = [tmc]
                ev4 = []
                for c4 in range(4):
                    bank = self.next_ps()
                    gi = self.ps.index(bank)
                    mw = [tsc, tkw, tbr, mk] + self.ps_guard[gi]
                    tok = p.step("tensor", (lambda e, b=bank, qt=qt, c4=c4:
                                 e.matmul(b[:], qa[0:65, qt * 128:(qt + 1) * 128],
                                          ka[0:65, c4 * 512:(c4 + 1) * 512],
                                          start=True, stop=True)), mw)
                    last_mm = tok
                    ew = [tok] + ([sr_guard[i % 2]] if (i % 2) in sr_guard else [])
                    te = p.step("scalar", (lambda e, b=bank, sr=sr, c4=c4: e.activation(
                        sr[:, c4 * 512:(c4 + 1) * 512], b[:], AF.Copy)), ew)
                    self.ps_guard[gi] = [te]
                    ev4.append(te)
                tmx = p.step("vector", (lambda e, sr=sr: _rmax(
                    e, stat[:, 0:1], sr, axis=AX.X)), ev4)
                sr_guard[i % 2] = tmx
                tM = p.step("vector", (lambda e, i=i: e.tensor_tensor(
                    Mtm[:, i:i + 1], stat[:, 0:1], stat[:, 5:6], ALU.subtract)),
                    [tmx, tmc])
                M_toks.append(tM)
            head_guard = [last_mm]
        SRW.free()

        tpm = p.step("vector", (lambda e: _rmax(e, t4, M3, axis=AX.X)), M_toks)
        tpm = p.step("vector", (lambda e: e.tensor_scalar(
            prt, t4, 1.01, None, ALU.mult)), [tpm])
        p.vspace(2)
        bank = self.next_ps()
        gi = self.ps.index(bank)
        tok = p.step("tensor", (lambda e, b=bank: e.matmul(
            b[0:1, 0:16], onescol, prt, start=True, stop=True)),
            [tpm, mk] + self.ps_guard[gi])
        thr = p.step("vector", (lambda e, b=bank: e.tensor_scalar(
            hirow, b[0:1, 0:16], 1.02, None, ALU.mult)), [tok])
        self.ps_guard[gi] = [thr]
        p.vspace(2)
        bank = self.next_ps()
        gi = self.ps.index(bank)
        tok = p.step("tensor", (lambda e, b=bank: e.matmul(
            b[:, 0:16], onesrow, hirow, start=True, stop=True)),
            [thr] + self.ps_guard[gi])
        thi = p.step("vector", (lambda e, b=bank: e.tensor_copy(hi, b[:, 0:16])), [tok])
        self.ps_guard[gi] = [thi]
        tlo = p.step("vector", (lambda e: e.memset(lo, 0.0)), M_toks)
        p.vspace(2)
        tmid = p.step("vector", (lambda e: e.tensor_scalar(
            mid, hi, 0.5, None, ALU.mult)), [thi])
        p.vspace(2)
        bdone = tmid
        for it in range(NBIS):
            midb = mid.unsqueeze(2).broadcast_to([128, 16, 16])
            tge = p.step("vector", (lambda e, mb=midb: e.tensor_tensor(
                GEB3, M3, mb, ALU.is_ge)), [bdone])
            tpr = p.step("vector", (lambda e: _rsum(e, prtf, GEB3, axis=AX.X)), [tge])
            tpr = p.step("vector", (lambda e: e.tensor_copy(prt, prtf)))
            bank = self.next_ps()
            gi = self.ps.index(bank)
            tok = p.step("tensor", (lambda e, b=bank: e.matmul(
                b[0:1, 0:16], onescol, prt, start=True, stop=True)),
                [tpr] + self.ps_guard[gi])
            tc38 = p.step("vector", (lambda e, b=bank: e.tensor_scalar(
                c38r, b[0:1, 0:16], float(U), None, ALU.is_ge)), [tok])
            self.ps_guard[gi] = [tc38]
            p.vspace(2)
            bank2 = self.next_ps()
            gi2b = self.ps.index(bank2)
            tok = p.step("tensor", (lambda e, b=bank2: e.matmul(
                b[:, 0:16], onesrow, c38r, start=True, stop=True)),
                [tc38] + self.ps_guard[gi2b])
            tu = p.step("vector", (lambda e: e.tensor_tensor(
                t4, mid, lo, ALU.subtract)), [tok])
            p.vspace(2)
            tu = p.step("vector", (lambda e, b=bank2: e.tensor_tensor(
                t4, b[:, 0:16], t4, ALU.mult)))
            p.vspace(2)
            tu = p.step("vector", (lambda e: e.tensor_tensor(lo, lo, t4, ALU.add)))
            p.vspace(2)
            tu = p.step("vector", (lambda e: e.tensor_tensor(
                t5, hi, mid, ALU.subtract)))
            p.vspace(2)
            tu2 = p.step("vector", (lambda e, b=bank2: e.tensor_tensor(
                t5, b[:, 0:16], t5, ALU.mult)))
            self.ps_guard[gi2b] = [tu2]
            p.vspace(2)
            tu = p.step("vector", (lambda e: e.tensor_tensor(hi, mid, t5, ALU.add)))
            p.vspace(2)
            tu = p.step("vector", (lambda e: e.tensor_tensor(mid, lo, hi, ALU.add)))
            p.vspace(2)
            bdone = p.step("vector", (lambda e: e.tensor_scalar(
                mid, mid, 0.5, None, ALU.mult)))
            p.vspace(2)
        p.vspace(4)
        if self.upto == "l1sel":
            self.dump({"mtm": Mtm, "cnt": cntb, "lo": lohi,
                       "qtm": qtm.rearrange("p a b -> p (a b)"),
                       "ktm": ktm.rearrange("p a b -> p (a b)")},
                      [bdone] + cnt_toks + M_toks + allq + allk)
            self._stopped = True
            return

        CTX2 = self.al.alloc("a1c2", 32 * 1024)
        OH = self.al.alloc("a1oh", 4096)
        ohb = OH.view([38, T], BF16)
        ctx2 = CTX2.view([128, NT, D], BF16)
        oh_rd = []
        ohb_rd = []
        kf_rd = []
        va_rd = []
        ctxn_rd = []
        pt2_rd = []
        qrs_rd = []
        qrf_rd = []
        vrow_rd = []
        sel_toks = []
        ctx_wr = [[] for _ in range(NT)]
        last_T = [bdone]
        for h in range(HEADS):
            tse = p.step("vector", (lambda e, h=h: e.tensor_scalar(
                selF, M3[:, h, :].opt(), lo[:, h:h + 1], None, ALU.is_ge)), [bdone])
            p.vspace(2)
            tsh = p.step("vector", (lambda e: e.tensor_copy(selh, selF)))
            tsa = p.step("vector", (lambda e, h=h: e.tensor_copy(
                SELALL[:, h * 16:(h + 1) * 16], selF)))
            sel_toks.append(tsa)
            tin = p.step("vector", (lambda e: e.tensor_scalar(
                invq, selF, 0.5, None, ALU.is_lt)))
            p.vspace(2)
            bank = self.next_ps()
            gi = self.ps.index(bank)
            tok = p.step("tensor", (lambda e, b=bank: e.matmul(
                b[0:1, 0:16], onescol, selh, start=True, stop=True)),
                [tsh] + self.ps_guard[gi])
            tcs = p.step("vector", (lambda e, b=bank: e.tensor_copy(
                scan[:, 0:16], b[0:1, 0:16])), [tok])
            self.ps_guard[gi] = [tcs]
            p.vspace(2)
            for si, sh in enumerate((1, 2, 4, 8)):
                a = scan[:, si * 16:(si + 1) * 16]
                bsl = scan[:, (si + 1) * 16:(si + 2) * 16]
                p.run("vector", (lambda e, a=a, b=bsl, sh=sh: e.tensor_copy(
                    b[:, 0:sh], a[:, 0:sh])))
                tss = p.step("vector", (lambda e, a=a, b=bsl, sh=sh: e.tensor_tensor(
                    b[:, sh:16], a[:, sh:16], a[:, 0:16 - sh], ALU.add)))
                p.vspace(2)
            tex = p.step("vector", (lambda e: e.tensor_tensor(
                scan[:, 80:96], scan[:, 64:80], scan[:, 0:16], ALU.subtract)))
            p.vspace(2)
            tex = p.step("vector", (lambda e: e.tensor_copy(
                exrow, scan[0:1, 80:96])))
            p.vspace(2)
            bank = self.next_ps()
            gi = self.ps.index(bank)
            tok = p.step("tensor", (lambda e, b=bank: e.matmul(
                b[:, 0:16], UT, selh, start=True, stop=False)),
                [tsh, tex, mk] + self.ps_guard[gi])
            tok = p.step("tensor", (lambda e, b=bank: e.matmul(
                b[:, 0:16], onesrow, exrow, start=False, stop=True)))
            trt = p.step("vector", (lambda e, b=bank: e.tensor_tensor(
                rtm, b[:, 0:16], selF, ALU.mult)), [tok])
            self.ps_guard[gi] = [trt]
            p.vspace(4)
            toh = None
            for qt in range(NT):
                toh = p.step("vector", (lambda e, qt=qt: e.tensor_scalar(
                    ohT3[:, qt, :], IOTA38, rtm[:, qt:qt + 1], None, ALU.is_equal)),
                    ([trt] + oh_rd) if qt == 0 else ())
            p.vspace(2)
            oht_toks = []
            for half in range(2):
                ptb = self.next_pt()
                gi2 = self.pt.index(ptb)
                w_ = [toh, self.const_tok] + ohb_rd + \
                    ([self.pt_guard[gi2]] if self.pt_guard[gi2] else [])
                tok = None
                for j in range(8):
                    qt = half * 8 + j
                    tok = p.step("tensor", (lambda e, qt=qt, j=j, b=ptb:
                                 e.transpose(b[0:38, j * 128:(j + 1) * 128],
                                             ohT3[:, qt, :].opt(), self.idn)),
                                 w_ if j == 0 else ())
                ev = p.step("vector", (lambda e, b=ptb, half=half: e.tensor_copy(
                    ohb[0:38, half * 1024:(half + 1) * 1024], b[0:38, :])), [tok])
                self.pt_guard[gi2] = ev
                oht_toks.append(ev)
            bank = self.next_ps()
            gi = self.ps.index(bank)
            mw = [toh] + allq + qrs_rd + self.ps_guard[gi]
            tok = None
            for qt in range(NT):
                tok = p.step("tensor", (lambda e, b=bank, qt=qt, h=h:
                             e.matmul(b[0:38, 0:64], ohT3[:, qt, :].opt(),
                                      qtm[:, qt, h * 64:(h + 1) * 64].opt(),
                                      start=(qt == 0), stop=(qt == NT - 1))),
                             mw if qt == 0 else ())
            oh_rd = [tok]
            tqr = p.step("vector", (lambda e, b=bank: e.tensor_copy(
                qrs, b[0:38, 0:64])), [tok])
            self.ps_guard[gi] = [tqr]
            ptb = self.next_pt()
            gi2 = self.pt.index(ptb)
            w_ = [tqr, self.const_tok] + qrf_rd + \
                ([self.pt_guard[gi2]] if self.pt_guard[gi2] else [])
            tok = p.step("tensor", (lambda e, b=ptb: e.transpose(
                b[0:64, 0:38], qrs, self.idn[0:38, 0:38])), w_)
            qrs_rd = [tok]
            tqf = p.step("scalar", (lambda e, b=ptb: e.activation(
                qrf, b[0:64, 0:38], AF.Copy, scale=0.125)), [tok])
            self.pt_guard[gi2] = tqf
            ktf = []
            for half in range(2):
                ptb = self.next_pt()
                gi2 = self.pt.index(ptb)
                w_ = allk + [self.const_tok] + kf_rd + \
                    ([self.pt_guard[gi2]] if self.pt_guard[gi2] else [])
                tok = None
                for j in range(8):
                    t = half * 8 + j
                    tok = p.step("tensor", (lambda e, t=t, j=j, b=ptb, h=h:
                                 e.transpose(b[0:64, j * 128:(j + 1) * 128],
                                             ktm[:, t, h * 64:(h + 1) * 64],
                                             self.idn)), w_ if j == 0 else ())
                ev = p.step("vector", (lambda e, b=ptb, half=half: e.tensor_copy(
                    ka[0:64, half * 1024:(half + 1) * 1024], b[0:64, :])), [tok])
                self.pt_guard[gi2] = ev
                ktf.append(ev)
            sp_toks = []
            for c4 in range(4):
                bank = self.next_ps()
                gi = self.ps.index(bank)
                tok = p.step("tensor", (lambda e, b=bank, c4=c4: e.matmul(
                    b[0:38, :], qrf, ka[0:64, c4 * 512:(c4 + 1) * 512],
                    start=True, stop=True)),
                    [tqf] + ktf + self.ps_guard[gi])
                kf_rd = [tok]
                qrf_rd = [tok]
                tev = p.step("vector", (lambda e, b=bank, c4=c4: e.tensor_copy(
                    srow2[:, c4 * 512:(c4 + 1) * 512], b[0:38, :])), [tok])
                self.ps_guard[gi] = [tev]
                sp_toks.append(tev)
            tmx = p.step("vector", (lambda e: _rmax(
                e, stt2[:, 0:1], srow2, axis=AX.X, negate=True)), sp_toks)
            texp = p.step("scalar", (lambda e: e.activation(
                pq2, srow2, AF.Exp, bias=stt2[:, 0:1])), [tmx] + pt2_rd)
            ptp_toks = []
            for half in range(2):
                ptb = self.next_pt()
                gi2 = self.pt.index(ptb)
                w_ = [texp, self.const_tok] + va_rd + \
                    ([self.pt_guard[gi2]] if self.pt_guard[gi2] else [])
                tok = None
                for j in range(8):
                    kt = half * 8 + j
                    tok = p.step("tensor", (lambda e, kt=kt, j=j, b=ptb:
                                 e.transpose(b[0:128, j * 128:j * 128 + 38],
                                             pq2[0:38, kt * 128:(kt + 1) * 128],
                                             self.idn[0:38, 0:38])),
                                 w_ if j == 0 else ())
                pv3 = ptb[:].rearrange("p (a q) -> p a q", a=8)[:, :, 0:38]
                ev = p.step("vector", (lambda e, pv=pv3, half=half: e.tensor_copy(
                    pt2[:, half * 8:(half + 1) * 8, :], pv)), [tok])
                self.pt_guard[gi2] = ev
                ptp_toks.append(ev)
            pt2_rd = []
            tva = p.step("vector", (lambda e, h=h: e.tensor_copy(
                va[:, :, 0:64], vtm[:, :, h * 64:(h + 1) * 64])), allv + va_rd)
            tva2 = p.step("vector", (lambda e: e.memset(va[:, :, 64:65], 1.0)), [tva])
            bank = self.next_ps()
            gi = self.ps.index(bank)
            mw = ptp_toks + [tva2] + ctxn_rd + self.ps_guard[gi]
            tok = None
            for kt in range(NT):
                tok = p.step("tensor", (lambda e, b=bank, kt=kt:
                             e.matmul(b[0:38, 0:65], pt2[:, kt, :].opt(),
                                      va[:, kt, :].opt(), start=(kt == 0),
                                      stop=(kt == NT - 1))), mw if kt == 0 else ())
            va_rd = [tok]
            pt2_rd = [tok]
            trc = p.step("vector", (lambda e, b=bank: e.reciprocal(
                stt2[:, 1:2], b[0:38, 64:65])), [tok])
            p.vspace(4)
            tcx = p.step("vector", (lambda e, b=bank: e.tensor_tensor(
                ctxn, b[0:38, 0:64], stt2[:, 1:2].broadcast_to([38, 64]),
                ALU.mult)), [trc])
            self.ps_guard[gi] = [tcx]
            bank = self.next_ps()
            gi = self.ps.index(bank)
            mw = allv + [mk] + self.ps_guard[gi]
            tok = None
            for kt in range(NT):
                tok = p.step("tensor", (lambda e, b=bank, kt=kt, h=h:
                             e.matmul(b[0:64, 0:1], vtm[:, kt, h * 64:(h + 1) * 64].opt(),
                                      onescol, start=(kt == 0),
                                      stop=(kt == NT - 1))), mw if kt == 0 else ())
            tvm = p.step("vector", (lambda e, b=bank: e.tensor_scalar(
                vmb, b[0:64, 0:1], 1.0 / T, None, ALU.mult)), [tok])
            self.ps_guard[gi] = [tvm]
            p.vspace(2)
            ptb = self.next_pt()
            gi2 = self.pt.index(ptb)
            w_ = [tvm, self.const_tok] + vrow_rd + \
                ([self.pt_guard[gi2]] if self.pt_guard[gi2] else [])
            tok = p.step("tensor", (lambda e, b=ptb: e.transpose(
                b[0:1, 0:64], vmb, self.idn[0:64, 0:64])), w_)
            tvr = p.step("vector", (lambda e, b=ptb: e.tensor_copy(
                vrow, b[0:1, 0:64])), [tok])
            self.pt_guard[gi2] = tvr
            bank = self.next_ps()
            gi = self.ps.index(bank)
            tok = p.step("tensor", (lambda e, b=bank: e.matmul(
                b[:, 0:64], onesrow, vrow, start=True, stop=True)),
                [tvr] + self.ps_guard[gi])
            vrow_rd = [tok]
            tvmr = p.step("vector", (lambda e, b=bank: e.tensor_copy(
                VMR, b[:, 0:64])), [tok])
            self.ps_guard[gi] = [tvmr]
            p.vspace(2)
            ohb_rd = []
            for qt in range(NT):
                bank = self.next_ps()
                gi = self.ps.index(bank)
                tok = p.step("tensor", (lambda e, b=bank, qt=qt: e.matmul(
                    b[:, 0:64], ohb[0:38, qt * 128:(qt + 1) * 128],
                    ctxn, start=True, stop=True)),
                    oht_toks + [tcx] + self.ps_guard[gi])
                ohb_rd.append(tok)
                tv1 = p.step("vector", (lambda e, qt=qt: e.tensor_scalar(
                    vtmp, VMR, invq[:, qt:qt + 1], None, ALU.mult)),
                    [tvmr, tin, tok])
                p.vspace(2)
                tv2 = p.step("vector", (lambda e, b=bank, qt=qt, h=h: e.tensor_tensor(
                    ctx2[:, qt, h * 64:(h + 1) * 64], b[:, 0:64], vtmp, ALU.add)))
                self.ps_guard[gi] = [tv2]
                ctx_wr[qt].append(tv2)
            ctxn_rd = ohb_rd
            last_T = [ohb_rd[-1]]

        for b_ in (QTM, KTM, VTM, KA, QA, VAB):
            b_.free()
        CTXF = self.al.alloc("a1cf", 32 * 1024)
        ctxf = CTXF.view([128, KC, T], BF16)
        ctxf_toks = []
        for t in range(NT):
            ev, rd = self.tm_to_fm(None, ctxf, t, ctx_wr[t] + last_T,
                                   src_cols=(lambda k, t=t: ctx2[:, t, k * 128:(k + 1) * 128]))
            ctxf_toks.append(ev)
        wo_src = self.dram["Wo"].ap().rearrange("(k p) n -> p k n", p=128)
        wo_tok = []
        for ng in range(2):
            wbuf = WST.view([128, KC, 512], BF16, ng * 8192)
            wo_tok.append(p.dma("gpsimd", wbuf,
                                wo_src[:, :, ng * 512:(ng + 1) * 512], [wtok[ng]]))
        tbo = p.dma("sync", SB.view([1, D], F32, 0), self.dram["bo"].ap(),
                    ctxf_toks + sel_toks)
        tbo = self.replicate_row(SB, None, 4096, [tbo], byte_off=0, have_row0=True)
        ga_buf, ga_rep, ga_tok = self.mod_rep(2, extra_waits=ctxf_toks)
        xsrc = self.dram[self.x_src].ap().rearrange("(t p) d -> p t d", p=128)
        xdst = self.dram["xres"].ap().rearrange("(t p) d -> p t d", p=128)
        XT = self.al.alloc("a1xt", 3 * 4096 + 2 * 2048)
        xg = {}
        rtmp_g = {}
        for t in range(NT):
            xt = XT.view([128, D], F32, (t % 3) * 4096)
            xw = [w for w in [self.x_tile_tok[t], xg.get(t % 3)] if w] + ctxf_toks
            xld = p.dma("sync", xt, xsrc[:, t, :], xw)
            tadd = None
            for ng in range(2):
                bank = self.next_ps()
                gi = self.ps.index(bank)
                wbuf = WST.view([128, KC, 512], BF16, ng * 8192)
                mmw = [wo_tok[ng]] + ctxf_toks + self.ps_guard[gi]
                tok = None
                for k in range(KC):
                    tok = p.step("tensor", (lambda e, b=bank, k=k, wb=wbuf, t=t:
                                 e.matmul(b[:], ctxf[:, k, t * 128:(t + 1) * 128].opt(),
                                          wb[:, k, :].opt(), start=(k == 0),
                                          stop=(k == KC - 1))), mmw if k == 0 else ())
                rtmp = XT.view([128, 512], F32, 3 * 4096 + ng * 2048)
                rw = [tok, ga_tok, tbo] + ([rtmp_g[ng]] if ng in rtmp_g else [])
                tmul = p.step("vector", (lambda e, b=bank, r=rtmp, ng=ng:
                              e.tensor_tensor(r, b[:], brep[:, ng * 512:(ng + 1) * 512],
                                              ALU.add)), rw)
                tmul = p.step("vector", (lambda e, r=rtmp, ng=ng: e.tensor_tensor(
                    r, r, ga_rep[:, ng * 512:(ng + 1) * 512], ALU.mult)))
                self.ps_guard[gi] = [tmul]
                tadd = p.step("vector", (lambda e, x=xt, r=rtmp, ng=ng: e.tensor_tensor(
                    x[:, ng * 512:(ng + 1) * 512], x[:, ng * 512:(ng + 1) * 512],
                    r, ALU.add)), [xld, tmul])
                rtmp_g[ng] = tadd
            tst = p.dma("sync", xdst[:, t, :], xt, [tadd])
            xg[t % 3] = tst
            self.x_tile_tok[t] = tst
        if self.upto == "l1attn":
            self.dump({"xres": self.dram["xres"].ap(), "mtm": Mtm,
                       "sel": SELALL, "cnt": cntb},
                      list(self.x_tile_tok) + [bdone] + sel_toks + cnt_toks)
            self._stopped = True
            return
        for b_ in (SB, WST, CTX2, CTXF, XT, ga_buf, OH):
            b_.free()

    def cc_barrier(self, src_row, dst_row, n, waits):
        p = self.p
        g = getattr(self, "cc_guard", [])
        t1 = p.dma("gpsimd", self.dram["cc_in"].ap()[:, 0:n], src_row, list(waits) + g)
        t2 = p.step("gpsimd", (lambda e: e.collective_compute(
            "AllReduce", ALU.max, replica_groups=[list(range(8))],
            ins=[self.dram["cc_in"].ap()], outs=[self.dram["cc_out"].ap()])), [t1])
        t3 = p.dma("gpsimd", dst_row, self.dram["cc_out"].ap()[:, 0:n], [t2])
        self.cc_guard = [t3]
        return t3

    def l1_ffn(self, HFM, hfm_toks):
        p = self.p
        hfm = HFM.view([128, KC, T], BF16)
        MAGIC = 12582912.0
        QM = 32767.0
        SB = self.al.alloc("f1sb", 8 * 1024)
        SCL = SB.view([128, 16], F32, 0)
        srow = SB.view([1, 16], F32, 1024)
        grow = SB.view([1, 16], F32, 1088)
        mx8 = SB.view([128, 8], F32, 1152)
        mxc = SB.view([128, 4], F32, 1216)
        mrow = SB.view([1, 128], F32, 1280)
        tcol = SB.view([128, 1], F32, 1792)
        AM = SB.view([128, 128], F32, 2048)
        NTMP = SB.view([128, 128], F32, 3584)
        SM = SB.view([128, 128], F32, 2560)
        RM = SB.view([128, 128], F32, 3072)
        tm1 = p.step("vector", (lambda e: _rmax(e, mx8, hfm, axis=AX.X,
                     apply_absolute_value=True)), hfm_toks)
        tm1 = p.step("vector", (lambda e: e.tensor_tensor(
            mx8[:, 0:4], mx8[:, 0:4], mx8[:, 4:8], ALU.max)))
        tm1 = p.step("vector", (lambda e: e.tensor_tensor(
            mx8[:, 0:2], mx8[:, 0:2], mx8[:, 2:4], ALU.max)))
        tm1 = p.step("vector", (lambda e: e.tensor_tensor(
            mxc[:, 0:1], mx8[:, 0:1], mx8[:, 1:2], ALU.max)))
        tg = p.dma("gpsimd", mrow, mxc[:, 0:1], [tm1])
        tm1 = p.step("vector", (lambda e: _rmax(e, srow[:, 0:1], mrow, axis=AX.X)), [tg])
        tb1 = self.cc_barrier(srow[:, 0:1], grow[:, 0:1], 1, [tm1])

        def mk_scale(gcol, scol, waits):
            t = p.step("vector", (lambda e: e.tensor_scalar(
                srow[:, scol:scol + 1], grow[:, gcol:gcol + 1], 1.0 / QM, 1e-12,
                ALU.mult, ALU.max)), waits)
            p.vspace(2)
            t = p.step("vector", (lambda e: e.reciprocal(
                srow[:, scol + 1:scol + 2], srow[:, scol:scol + 1])), [t])
            p.vspace(4)
            return t

        ts1 = mk_scale(0, 0, [tb1])
        tr1 = p.dma("gpsimd", SB.view([1, 16], F32, 0), srow, [ts1])
        tr1 = self.replicate_row(SB, None, 64, [tr1], have_row0=True)
        QS = self.al.alloc("f1qs", 2 * 8192)
        hq_toks = []
        sc_g = {}
        for k in range(KC):
            scr = QS.view([128, T], F32, (k % 2) * 8192)
            w_ = [tr1] + (sc_g.get(k % 2, []))
            t1 = p.step("vector", (lambda e, k=k, s=scr: e.tensor_scalar(
                s, hfm[:, k, :], SCL[:, 1:2], MAGIC, ALU.mult, ALU.add)), w_)
            t2 = p.step("vector", (lambda e, k=k, s=scr: e.tensor_scalar(
                hfm[:, k, :], s, MAGIC, SCL[:, 0:1], ALU.subtract, ALU.mult)), [t1])
            sc_g[k % 2] = [t2]
            hq_toks.append(t2)
        WTL = self.al.alloc("f1wt", 16 * 1024)
        WQB = self.al.alloc("f1wq", 8 * 1024)
        WPP = self.al.alloc("f1wp", 8 * 1024)
        OTL = self.al.alloc("f1ot", 2 * 2048)
        DSC = self.al.alloc("f1ds", 16 * 1024)
        wtl = WTL.view([128, 4, 1024], F32)
        wtl3 = WTL.view([128, 128, 32], F32)
        wqb = WQB.view([128, 4, 1024], BF16)
        wqb3 = WQB.view([128, 128, 32], BF16)
        dsc3 = DSC.view([128, 128, 32], F32)
        wpp = WPP.view([128, KC, 512], BF16)
        up_gate_toks = []
        fold_toks = []
        wtl_g = []
        otl_g = {}
        for wi, (wname, hbname, mcol) in enumerate((("w_up", "hb_up", 1),
                                                    ("w_gate", "hb_gate", 2))):
            wsrc = self.dram[wname].ap().rearrange("(ot p) i -> p ot i", p=128)
            hbdst = self.dram[hbname].ap().rearrange("(tt p) (og c) -> p tt og c",
                                                     p=128, c=512)
            tmz = p.step("vector", (lambda e, mcol=mcol: e.memset(
                mxc[:, mcol:mcol + 1], 0.0)), hfm_toks)
            for og in range(6):
                tw = p.dma("sync", wtl, wsrc[:, og * 4:(og + 1) * 4, :],
                           [tr1] + wtl_g + hq_toks)
                tam = p.step("vector", (lambda e: _rmax(e, AM, wtl3, axis=AX.X,
                             apply_absolute_value=True)), [tw])
                tsm = p.step("vector", (lambda e: e.tensor_scalar(
                    SM, AM, 1.0 / 7.0, 1e-12, ALU.mult, ALU.max)), [tam])
                p.vspace(2)
                trm = p.step("vector", (lambda e: e.reciprocal(RM, SM)), [tsm])
                p.vspace(4)
                trm = p.step("vector", (lambda e: e.tensor_tensor(
                    NTMP, SM, RM, ALU.mult)), [trm])
                trm = p.step("vector", (lambda e: e.tensor_scalar(
                    NTMP, NTMP, -1.0, 2.0, ALU.mult, ALU.add)), [trm])
                trm = p.step("vector", (lambda e: e.tensor_tensor(
                    RM, RM, NTMP, ALU.mult)), [trm])
                p.vspace(2)
                rmb = RM.rearrange("p (a b) -> p a b", b=1).broadcast_to([128, 128, 32])
                smb = SM.rearrange("p (a b) -> p a b", b=1).broadcast_to([128, 128, 32])
                t1 = p.step("vector", (lambda e, rmb=rmb: e.tensor_tensor(
                    dsc3, wtl3, rmb, ALU.mult)), [trm])
                t2 = p.step("vector", (lambda e: e.tensor_scalar(
                    dsc3, dsc3, MAGIC, MAGIC, ALU.add, ALU.subtract)), [t1])
                t3 = p.step("vector", (lambda e, smb=smb: e.tensor_tensor(
                    wqb3, dsc3, smb, ALU.mult)), [t2])
                wtl_g = [t3]
                tp_toks = []
                for ot in range(4):
                    ptb = self.next_pt()
                    gi2 = self.pt.index(ptb)
                    w_ = [t3, self.const_tok] + \
                        ([self.pt_guard[gi2]] if self.pt_guard[gi2] else [])
                    tok = None
                    for ic in range(KC):
                        tok = p.step("tensor", (lambda e, ot=ot, ic=ic, b=ptb:
                                     e.transpose(b[:, ic * 128:(ic + 1) * 128],
                                                 wqb[:, ot, ic * 128:(ic + 1) * 128].opt(),
                                                 self.idn)), w_ if ic == 0 else ())
                    ev = p.step("vector", (lambda e, b=ptb, ot=ot: e.tensor_copy(
                        wpp[:, :, ot * 128:(ot + 1) * 128],
                        b[:].rearrange("p (a q) -> p a q", a=8))), [tok])
                    self.pt_guard[gi2] = ev
                    tp_toks.append(ev)
                last_sv = None
                for tt_ in range(NT):
                    bank = self.next_ps()
                    gi = self.ps.index(bank)
                    mw = tp_toks + hq_toks + self.ps_guard[gi]
                    tok = None
                    for ic in range(KC):
                        tok = p.step("tensor", (lambda e, b=bank, ic=ic, tt_=tt_:
                                     e.matmul(b[:], hfm[:, ic, tt_ * 128:(tt_ + 1) * 128].opt(),
                                              wpp[:, ic, :].opt(), start=(ic == 0),
                                              stop=(ic == KC - 1))), mw if ic == 0 else ())
                    otl = OTL.view([128, 512], F32, (tt_ % 2) * 2048)
                    ow = [tok] + (otl_g.get(tt_ % 2, []))
                    tev = p.step("scalar", (lambda e, b=bank, o=otl: e.activation(
                        o, b[:], AF.Copy)), ow)
                    self.ps_guard[gi] = [tev]
                    tmx = p.step("vector", (lambda e, o=otl: _rmax(
                        e, tcol, o, axis=AX.X, apply_absolute_value=True)),
                        [tev, tmz])
                    tmx = p.step("vector", (lambda e, mcol=mcol: e.tensor_tensor(
                        mxc[:, mcol:mcol + 1], mxc[:, mcol:mcol + 1], tcol,
                        ALU.max)), [tmx])
                    tst = p.dma("sync", hbdst[:, tt_, og, :], otl, [tev])
                    otl_g[tt_ % 2] = [tmx, tst]
                    last_sv = tst
                    up_gate_toks.append(tst)
            tg = p.dma("gpsimd", mrow, mxc[:, mcol:mcol + 1],
                       [t for pr in otl_g.values() for t in pr])
            fold_toks.append(p.step("vector", (lambda e, mcol=mcol: _rmax(
                e, srow[:, mcol:mcol + 1], mrow, axis=AX.X)), [tg]))
        tb2 = self.cc_barrier(srow[:, 1:3], grow[:, 1:3], 2, fold_toks)
        tsu = mk_scale(1, 2, [tb2])
        tsg2 = mk_scale(2, 4, [tsu])
        tr2 = p.dma("gpsimd", SB.view([1, 16], F32, 0), srow, [tsg2])
        tr2 = self.replicate_row(SB, None, 64, [tr2], have_row0=True)
        HFM.free()
        WTL.free()
        WQB.free()
        WPP.free()
        OTL.free()
        QS.free()
        GT = self.al.alloc("f1gt", 2 * 12 * 1024)
        SA = self.al.alloc("f1sa", 12 * 1024)
        SBq = self.al.alloc("f1sb2", 12 * 1024)
        SCq = self.al.alloc("f1sc", 12 * 1024)
        gsrc = self.dram["hb_gate"].ap().rearrange("(tt p) i -> p tt i", p=128)
        usrc = self.dram["hb_up"].ap().rearrange("(tt p) i -> p tt i", p=128)
        p2dst = self.dram["hb_p2"].ap().rearrange("(tt p) i -> p tt i", p=128)
        p3dst = self.dram["hb_p3"].ap().rearrange("(tt p) i -> p tt i", p=128)
        DI3 = DI
        t3_toks = []
        gld_g = {}
        tmz3 = p.step("vector", (lambda e: e.memset(mxc[:, 3:4], 0.0)), [tr2])
        sab_g = []
        for tt_ in range(NT):
            gt = GT.view([128, DI3], F32, (tt_ % 2) * 12288)
            sa_ = SA.view([128, DI3], F32)
            sb_ = SBq.view([128, DI3], F32)
            sc_ = SCq.view([128, DI3], F32)
            gw = [tr2] + (gld_g.get(tt_ % 2, []))
            tld = p.dma("sync", gt, gsrc[:, tt_, :], gw + up_gate_toks)
            t1 = p.step("vector", (lambda e, gt=gt, s=sa_: e.tensor_scalar(
                s, gt, SCL[:, 5:6], MAGIC, ALU.mult, ALU.add)), [tld] + sab_g)
            t2 = p.step("vector", (lambda e, s=sa_, o=sb_: e.tensor_scalar(
                o, s, MAGIC, SCL[:, 4:5], ALU.subtract, ALU.mult)), [t1])
            tsig = p.step("scalar", (lambda e, i=sb_, o=sa_: e.activation(
                o, i, AF.Sigmoid)), [t2])
            t4 = p.step("vector", (lambda e, s=sa_, o=sc_: e.tensor_scalar(
                o, s, 65536.0, MAGIC, ALU.mult, ALU.add)), [tsig])
            t5 = p.step("vector", (lambda e, s=sc_: e.tensor_scalar(
                s, s, MAGIC, 65535.0, ALU.subtract, ALU.min)), [t4])
            t6 = p.step("vector", (lambda e, s=sc_: e.tensor_scalar(
                s, s, 1.0 / 65536.0, None, ALU.mult)), [t5])
            t7 = p.step("vector", (lambda e, a=sc_, b=sb_: e.tensor_tensor(
                a, a, b, ALU.mult)), [t6])
            sab_g = [t7]
            tmx = p.step("vector", (lambda e, s=sc_: _rmax(
                e, tcol, s, axis=AX.X, apply_absolute_value=True)), [t7, tmz3])
            tmx = p.step("vector", (lambda e: e.tensor_tensor(
                mxc[:, 3:4], mxc[:, 3:4], tcol, ALU.max)), [tmx])
            tst = p.dma("sync", p2dst[:, tt_, :], sc_, [t7])
            gld_g[tt_ % 2] = [tst]
            sab_g.append(tmx)
            sab_g.append(tst)
            t3_toks.append(tst)
        tg = p.dma("gpsimd", mrow, mxc[:, 3:4], sab_g)
        tm3 = p.step("vector", (lambda e: _rmax(e, srow[:, 6:7], mrow, axis=AX.X)), [tg])
        tb3 = self.cc_barrier(srow[:, 6:7], grow[:, 6:7], 1, [tm3])
        ts3 = mk_scale(6, 6, [tb3])
        tr3 = p.dma("gpsimd", SB.view([1, 16], F32, 0), srow, [ts3])
        tr3 = self.replicate_row(SB, None, 64, [tr3], have_row0=True)
        WDN = self.al.alloc("f1wd", 48 * 1024)
        wdn = WDN.view([128, 24, 1024], BF16)
        WD1 = self.al.alloc("f1wd1", 12 * 1024)
        WD2 = self.al.alloc("f1wd2", 8 * 1024)
        AMD = SB.view([128, 96], F32, 2048)
        SMD = SB.view([128, 96], F32, 2560)
        RMD = SB.view([128, 96], F32, 3072)
        wdsrc = self.dram["w_down"].ap().rearrange("(dt p) i -> p dt i", p=128)
        wd1 = WD1.view([128, DI3], F32)
        wd13 = WD1.view([128, 96, 32], F32)
        wdq = WD2.view([128, DI3], BF16)
        wdq3 = WD2.view([128, 96, 32], BF16)
        DSC3b = DSC.view([128, 96, 32], F32)
        wd_toks = []
        twg = []
        for dt in range(8):
            tw = p.dma("sync", wd1, wdsrc[:, dt, :], twg + t3_toks)
            tam = p.step("vector", (lambda e: _rmax(e, AMD, wd13, axis=AX.X,
                         apply_absolute_value=True)), [tw])
            tsm = p.step("vector", (lambda e: e.tensor_scalar(
                SMD, AMD, 1.0 / 7.0, 1e-12, ALU.mult, ALU.max)), [tam])
            p.vspace(2)
            trm = p.step("vector", (lambda e: e.reciprocal(RMD, SMD)), [tsm])
            p.vspace(4)
            trm = p.step("vector", (lambda e: e.tensor_tensor(
                NTMP[:, 0:96], SMD, RMD, ALU.mult)), [trm])
            trm = p.step("vector", (lambda e: e.tensor_scalar(
                NTMP[:, 0:96], NTMP[:, 0:96], -1.0, 2.0, ALU.mult, ALU.add)), [trm])
            trm = p.step("vector", (lambda e: e.tensor_tensor(
                RMD, RMD, NTMP[:, 0:96], ALU.mult)), [trm])
            p.vspace(2)
            rmb = RMD.rearrange("p (a b) -> p a b", b=1).broadcast_to([128, 96, 32])
            smb = SMD.rearrange("p (a b) -> p a b", b=1).broadcast_to([128, 96, 32])
            t1 = p.step("vector", (lambda e, rmb=rmb: e.tensor_tensor(
                DSC3b, wd13, rmb, ALU.mult)), [trm])
            t2 = p.step("vector", (lambda e: e.tensor_scalar(
                DSC3b, DSC3b, MAGIC, MAGIC, ALU.add, ALU.subtract)), [t1])
            t3 = p.step("vector", (lambda e, smb=smb: e.tensor_tensor(
                wdq3, DSC3b, smb, ALU.mult)), [t2])
            twg = [t3]
            for g8 in range(3):
                ptb = self.next_pt()
                gi2 = self.pt.index(ptb)
                w_ = [t3, self.const_tok] + \
                    ([self.pt_guard[gi2]] if self.pt_guard[gi2] else [])
                tok = None
                for j in range(8):
                    ic = g8 * 8 + j
                    tok = p.step("tensor", (lambda e, ic=ic, j=j, b=ptb:
                                 e.transpose(b[:, j * 128:(j + 1) * 128],
                                             wdq[:, ic * 128:(ic + 1) * 128],
                                             self.idn)), w_ if j == 0 else ())
                ev = p.step("vector", (lambda e, b=ptb, g8=g8, dt=dt: e.tensor_copy(
                    wdn[:, g8 * 8:(g8 + 1) * 8, dt * 128:(dt + 1) * 128],
                    b[:].rearrange("p (a q) -> p a q", a=8))), [tok])
                self.pt_guard[gi2] = ev
                twg.append(ev)
                wd_toks.append(ev)
        WD1.free()
        WD2.free()
        UPT = self.al.alloc("f1up", 2 * 12 * 1024)
        o_toks = []
        uld_g = {}
        tmz4 = p.step("vector", (lambda e: e.memset(mxc[:, 0:1], 0.0)), [tr3])
        ob_g = []
        for tt_ in range(NT):
            gt = GT.view([128, DI3], F32, (tt_ % 2) * 12288)
            ut = UPT.view([128, DI3], F32, (tt_ % 2) * 12288)
            sa_ = SA.view([128, DI3], F32)
            sb_ = SBq.view([128, DI3], F32)
            sc_ = SCq.view([128, DI3], F32)
            gw = [tr3] + (uld_g.get(tt_ % 2, []))
            tld = p.dma("sync", gt, self.dram["hb_p2"].ap().rearrange(
                "(tt p) i -> p tt i", p=128)[:, tt_, :], gw + t3_toks + wd_toks)
            tlu = p.dma("sync", ut, usrc[:, tt_, :], gw + t3_toks + wd_toks)
            t1 = p.step("vector", (lambda e, gt=gt, s=sa_: e.tensor_scalar(
                s, gt, SCL[:, 7:8], MAGIC, ALU.mult, ALU.add)), [tld] + ob_g)
            t2 = p.step("vector", (lambda e, s=sa_: e.tensor_scalar(
                s, s, MAGIC, SCL[:, 6:7], ALU.subtract, ALU.mult)), [t1])
            t1b = p.step("vector", (lambda e, ut=ut, s=sb_: e.tensor_scalar(
                s, ut, SCL[:, 3:4], MAGIC, ALU.mult, ALU.add)), [tlu])
            t2b = p.step("vector", (lambda e, s=sb_: e.tensor_scalar(
                s, s, MAGIC, SCL[:, 2:3], ALU.subtract, ALU.mult)), [t1b])
            t5 = p.step("vector", (lambda e, a=sc_, x=sa_, y=sb_: e.tensor_tensor(
                a, x, y, ALU.mult)), [t2, t2b])
            ob_g = [t5]
            tmx = p.step("vector", (lambda e, s=sc_: _rmax(
                e, tcol, s, axis=AX.X, apply_absolute_value=True)), [t5, tmz4])
            tmx = p.step("vector", (lambda e: e.tensor_tensor(
                mxc[:, 0:1], mxc[:, 0:1], tcol, ALU.max)), [tmx])
            tst = p.dma("sync", p3dst[:, tt_, :], sc_, [t5])
            uld_g[tt_ % 2] = [tst]
            ob_g.append(tmx)
            ob_g.append(tst)
            o_toks.append(tst)
        tg = p.dma("gpsimd", mrow, mxc[:, 0:1], ob_g)
        tm4 = p.step("vector", (lambda e: _rmax(e, srow[:, 8:9], mrow, axis=AX.X)), [tg])
        tb4 = self.cc_barrier(srow[:, 8:9], grow[:, 8:9], 1, [tm4])
        ts4 = mk_scale(8, 8, [tb4])
        tr4 = p.dma("gpsimd", SB.view([1, 16], F32, 0), srow, [ts4])
        tr4 = self.replicate_row(SB, None, 64, [tr4], have_row0=True)
        OQ = self.al.alloc("f1oq", 6 * 1024)
        OQF = self.al.alloc("f1oqf", 6 * 1024)
        oq = OQ.view([128, DI3], BF16)
        oqf = OQF.view([128, 24, 128], BF16)
        gb_buf, gb_rep, gb_tok = self.mod_rep(5, extra_waits=o_toks)
        XT = self.al.alloc("f1xt", 2 * 4096 + 2 * 2048)
        xsrc = self.dram[self.x_src].ap().rearrange("(t p) d -> p t d", p=128)
        odst = self.out_t.ap().rearrange("(t p) d -> p t d", p=128)
        out_toks = []
        oq_rd = []
        oqf_rd = []
        xg = {}
        rtmp_g = {}
        for tt_ in range(NT):
            gt = GT.view([128, DI3], F32, (tt_ % 2) * 12288)
            sa_ = SA.view([128, DI3], F32)
            gw = [tr4] + (uld_g.get(tt_ % 2, []))
            tld = p.dma("sync", gt, self.dram["hb_p3"].ap().rearrange(
                "(tt p) i -> p tt i", p=128)[:, tt_, :], gw + o_toks)
            t1 = p.step("vector", (lambda e, gt=gt, s=sa_: e.tensor_scalar(
                s, gt, SCL[:, 9:10], MAGIC, ALU.mult, ALU.add)),
                [tld] + ob_g + oq_rd)
            t2 = p.step("vector", (lambda e, s=sa_: e.tensor_scalar(
                oq, s, MAGIC, SCL[:, 8:9], ALU.subtract, ALU.mult)), [t1])
            ob_g = [t2]
            uld_g[tt_ % 2] = [t1]
            tpq = []
            for g8 in range(3):
                ptb = self.next_pt()
                gi2 = self.pt.index(ptb)
                w_ = [t2, self.const_tok] + oqf_rd + \
                    ([self.pt_guard[gi2]] if self.pt_guard[gi2] else [])
                tok = None
                for j in range(8):
                    ic = g8 * 8 + j
                    tok = p.step("tensor", (lambda e, ic=ic, j=j, b=ptb:
                                 e.transpose(b[:, j * 128:(j + 1) * 128],
                                             oq[:, ic * 128:(ic + 1) * 128],
                                             self.idn)), w_ if j == 0 else ())
                ev = p.step("vector", (lambda e, b=ptb, g8=g8: e.tensor_copy(
                    oqf[:, g8 * 8:(g8 + 1) * 8, :],
                    b[:].rearrange("p (a q) -> p a q", a=8))), [tok])
                self.pt_guard[gi2] = ev
                tpq.append(ev)
            oq_rd = [tpq[-1]]
            xt = XT.view([128, D], F32, (tt_ % 2) * 4096)
            xw = [w for w in [self.x_tile_tok[tt_], xg.get(tt_ % 2)] if w]
            xld = p.dma("sync", xt, xsrc[:, tt_, :], xw + [t2])
            tadd = None
            mmtok = None
            for ng in range(2):
                bank = self.next_ps()
                gi = self.ps.index(bank)
                mw = tpq + wd_toks + self.ps_guard[gi]
                tok = None
                for ic in range(24):
                    tok = p.step("tensor", (lambda e, b=bank, ic=ic, ng=ng:
                                 e.matmul(b[:], oqf[:, ic, :].opt(),
                                          wdn[:, ic, ng * 512:(ng + 1) * 512].opt(),
                                          start=(ic == 0), stop=(ic == 23))),
                                 mw if ic == 0 else ())
                mmtok = tok
                rtmp = XT.view([128, 512], F32, 2 * 4096 + ng * 2048)
                rw = [tok, gb_tok] + ([rtmp_g[ng]] if ng in rtmp_g else [])
                tmul = p.step("vector", (lambda e, b=bank, r=rtmp, ng=ng:
                              e.tensor_tensor(r, b[:], gb_rep[:, ng * 512:(ng + 1) * 512],
                                              ALU.mult)), rw)
                self.ps_guard[gi] = [tmul]
                tadd = p.step("vector", (lambda e, x=xt, r=rtmp, ng=ng: e.tensor_tensor(
                    x[:, ng * 512:(ng + 1) * 512], x[:, ng * 512:(ng + 1) * 512],
                    r, ALU.add)), [xld, tmul])
                rtmp_g[ng] = tadd
            oqf_rd = [mmtok]
            tst = p.dma("sync", odst[:, tt_, :], xt, [tadd])
            xg[tt_ % 2] = tst
            out_toks.append(tst)
        p.run("sync", (lambda e: e.nop()), out_toks)
        if self.upto == "l1ffn":
            self.dump({"scl": SCL, "grow": grow, "wdn": wdn.rearrange("p a b -> p (a b)"),
                       "hbup": self.dram["hb_up"].ap(),
                       "hbgate": self.dram["hb_gate"].ap(),
                       "hbp2": self.dram["hb_p2"].ap(),
                       "hbp3": self.dram["hb_p3"].ap()},
                      out_toks + [tr4])

    def store_out(self, waits):
        p = self.p
        xv = self.X.view([128, NT, D], F32)
        dst = self.out_t.ap().rearrange("(t p) d -> p t d", p=128)
        tok = p.dma("sync", dst, xv, waits)
        p.run("sync", lambda e: e.nop(), [tok])

    def dump(self, aps_named, waits):
        p = self.p
        toks = []
        for name, ap in aps_named.items():
            shp = list(ap.shape)
            flat = [shp[0], int(np.prod(shp[1:]))] if len(shp) > 2 else shp
            t = self.dout("dbg_" + name, flat, ap.dtype)
            src = ap
            if len(shp) > 2:
                dims = " ".join(f"d{i}" for i in range(1, len(shp)))
                src = ap.rearrange(f"p {dims} -> p ({dims})")
            toks.append(p.dma("sync", t.ap(), src, waits))
        p.run("sync", lambda e: e.nop(), toks)


def build_program(upto=None):
    nc = bass.Bass("TRN2", target_bir_lowering=False, detect_race_conditions=False)
    ctx = ExitStack()
    b = Build(nc, ctx, upto)
    b.declare_io()
    b.load_x()
    b.mod(0)
    if upto == "mod0":
        b.dump({"modc": b.MODC.view([6, D], F32)}, b.mod_tok)
        b.p.realize(); ctx.close(); return nc, b
    HFM, ev = b.ln_mod(1, 0)
    if upto == "ln0a":
        b.dump({"hfm": HFM.view([128, KC, T], BF16)}, ev)
        b.p.realize(); ctx.close(); return nc, b
    HFM = b.l0_attn(HFM, ev)
    if upto in ("l0qkln", "l0ctx", "l0pat", "l0proj", "l0stat"):
        b.p.realize(); ctx.close(); return nc, b
    if upto == "l0attn":
        b.dump({"xres": b.dram["xres"].ap()}, b.x_tile_tok)
        b.p.realize(); ctx.close(); return nc, b
    HFM2, ev2 = b.ln_mod(4, 3)
    b.l0_ffn(HFM2, ev2)
    if upto == "l0ffn":
        b.dump({"xres": b.dram["xres"].ap()}, b.x_tile_tok)
        b.p.realize(); ctx.close(); return nc, b
    b.mod(1, extra_waits=[t for t in b.x_tile_tok if t])
    if upto == "mod1":
        b.dump({"modc": b.MODC.view([6, D], F32)}, b.mod_tok)
        b.p.realize(); ctx.close(); return nc, b
    HFM3, ev3 = b.ln_mod(1, 0)
    if upto == "l1h":
        b.dump({"hfm": HFM3.view([128, KC, T], BF16),
                "xres": b.dram["xres"].ap()}, ev3 + list(b.x_tile_tok))
        b.p.realize(); ctx.close(); return nc, b
    b.l1_attn(HFM3, ev3)
    if upto in ("l1attn", "l1sel"):
        b.p.realize(); ctx.close(); return nc, b
    HFM4, ev4 = b.ln_mod(4, 3)
    b.l1_ffn(HFM4, ev4)
    b.p.realize(); ctx.close(); return nc, b


def _stage_inputs(kw):
    n = NCORES
    f32 = np.float32
    def rs(a, shape):
        return np.ascontiguousarray(np.asarray(a).reshape(shape))
    common = {
        "L0_wmod": rs(kw["L0_wmod"], (D, 6 * D)),
        "W_qk": rs(kw["L0_attn_W_qk"], (D, 2 * D)),
        "W_v": rs(kw["L0_attn_W_v"], (D, D)),
        "W_out": rs(kw["L0_attn_W_out"], (D, D)),
        "q_scale": rs(kw["L0_attn_q_scale"], (1, HD)),
        "q_bias": rs(kw["L0_attn_q_bias"], (1, HD)),
        "k_scale": rs(kw["L0_attn_k_scale"], (1, HD)),
        "k_bias": rs(kw["L0_attn_k_bias"], (1, HD)),
        "L1_wmod": rs(kw["L1_wmod"], (D, 6 * D)),
        "Wq": rs(kw["L1_attn_Wq"], (D, D)), "bq": rs(kw["L1_attn_bq"], (1, D)),
        "Wk": rs(kw["L1_attn_Wk"], (D, D)), "bk": rs(kw["L1_attn_bk"], (1, D)),
        "Wv": rs(kw["L1_attn_Wv"], (D, D)), "bv": rs(kw["L1_attn_bv"], (1, D)),
        "Wo": rs(kw["L1_attn_Wo"], (D, D)), "bo": rs(kw["L1_attn_bo"], (1, D)),
        "sample_idx": rs(kw["L1_attn_sample_idx"], (1, T)),
        "w_gate": rs(kw["L1_ffn_w_gate"], (DI, D)),
        "w_up": rs(kw["L1_ffn_w_up"], (DI, D)),
        "w_down": rs(kw["L1_ffn_w_down"], (D, DI)),
    }
    for pre, rp in (("fq", "q"), ("fk", "k"), ("fv", "v")):
        common[pre + "_nw"] = rs(kw[f"L0_ffn_{rp}_node_w"], (15, D))
        common[pre + "_nb"] = rs(kw[f"L0_ffn_{rp}_node_b"], (1, 15))
        common[pre + "_w1"] = np.ascontiguousarray(np.asarray(kw[f"L0_ffn_{rp}_w1"]).reshape(16, D, HD).transpose(1, 0, 2).reshape(D, D))
        common[pre + "_b1"] = rs(kw[f"L0_ffn_{rp}_b1"], (1, 16 * HD))
        common[pre + "_w2"] = rs(kw[f"L0_ffn_{rp}_w2"], (16 * HD, D))
        common[pre + "_b2"] = rs(kw[f"L0_ffn_{rp}_b2"], (16, D))
    x = np.asarray(kw["L0_attn_inputs"])
    c0 = np.asarray(kw["L0_c"])
    c1 = np.asarray(kw["L1_c"])
    in_maps = []
    for i in range(n):
        m = dict(common)
        m["x"] = np.ascontiguousarray(x[i])
        m["L0_c"] = rs(c0[i], (1, D))
        m["L1_c"] = rs(c1[i], (1, D))
        in_maps.append(m)
    return in_maps


def kernel(**kw):
    nc, b = build_program()
    in_maps = _stage_inputs(kw)
    res = run_bass_kernel_spmd(nc, in_maps, core_ids=list(range(NCORES)))
    return np.stack([np.asarray(r["out"]) for r in res.results], 0)
